# version 05
# baseline (speedup 1.0000x reference)
import numpy as np
import ml_dtypes
import concourse.bass as bass
import concourse.mybir as mybir
from concourse.bass_utils import run_bass_kernel_spmd

F32 = mybir.dt.float32
BF16 = mybir.dt.bfloat16
FP8 = mybir.dt.float8e4
I32 = mybir.dt.int32
AF = mybir.ActivationFunctionType
ALU = mybir.AluOpType
AX = mybir.AxisListType

N = 4096
C = 1024
HEADS = 16
INNER = 4096
NT = N // 128
THETA = 10000.0

ENGINES = ("sync", "tensor", "vector", "scalar", "gpsimd")


class Plan:

    def __init__(self, nc):
        self.nc = nc
        self.q = {e: [] for e in ENGINES}
        self.sems = {}
        self.cnt = {}

    def emit(self, eng, make, inc=None, amt=1):
        if inc is not None:
            self.cnt[inc] += amt
            val = self.cnt[inc]
        else:
            val = None

        def fn(e, make=make, inc=inc, amt=amt):
            r = make(e)
            if inc is not None:
                r.then_inc(self.sems[inc], amt)
        self.q[eng].append(fn)
        return val

    def wait(self, eng, sem, val):
        if val is None or val <= 0:
            return
        self.q[eng].append(lambda e, s=sem, v=val: e.wait_ge(self.sems[s], v))

    def now(self, sem):
        return self.cnt[sem]

    def run(self):
        with self.nc.Block() as block:
            for name in ENGINES:
                fns = self.q[name]

                def body(eng, fns=fns):
                    for fn in fns:
                        fn(eng)
                getattr(block, name)(body)


def build_program(nc, n_tiles=NT, debug=False, phases="PABD", dbg_t=0, barriers=True):
    def din(name, shape, dt):
        return nc.dram_tensor(name, shape, dt, kind="ExternalInput").ap()

    def dout(name, shape, dt):
        return nc.dram_tensor(name, shape, dt, kind="ExternalOutput").ap()

    def dint(name, shape, dt):
        return nc.dram_tensor(name, shape, dt, kind="Internal").ap()

    x_d = din("x", [N, C], F32)
    wq_d = din("wq", [1025, C], F32)
    wk_d = din("wk", [1025, C], F32)
    wv_d = din("wv", [1025, C], F32)
    wg_d = din("wg", [C, INNER], F32)
    wu_d = din("wu", [C, INNER], F32)
    wd_d = din("wd", [INNER, C], F32)
    gamma_d = din("gamma", [1, C], F32)
    beta_d = din("beta", [1, C], F32)
    tabs_d = din("tabs", [N, C], F32)
    idn_d = din("idn", [128, 128], BF16)
    idnf_d = din("idnf", [128, 128], F32)
    ones1b_d = din("ones1b", [1, 128], BF16)
    onesf_d = din("onesf", [128, 1], F32)
    ones1f_d = din("ones1f", [1, 128], F32)

    y_d = dout("y", [N, C], F32)
    x2_d = dout("x2d", [N, C], F32) if debug else dint("x2d", [N, C], F32)
    wg8_d = dint("wg8d", [8, 8, 128, 512], FP8)
    wu8_d = dint("wu8d", [8, 8, 128, 512], FP8)

    dbg = {}
    if debug:
        for nm in ["a_x", "a_stats", "a_lnb", "a_kg", "a_ropeb", "a_vb", "kvS", "kmB",
                   "b_qb", "b_zpre", "b_zt", "b_attn", "b_res", "b_t2", "b_x2",
                   "d_ln3", "d_xq", "d_gu", "d_guq", "d_t4", "d_y"]:
            shape = [128, 8 * 128] if nm == "kvS" else (
                [128, 16] if nm in ("kmB", "b_zt", "b_zpre", "a_stats") else (
                    [128, INNER] if nm in ("d_gu", "d_guq") else [128, C]))
            dbg[nm] = dout("dbg_" + nm, shape, F32)

    from contextlib import ExitStack
    with ExitStack() as _st:
        _E = _st.enter_context
        wqb = _E(nc.sbuf_tensor("wqb", [128, 8 * C], BF16))
        wkb = _E(nc.sbuf_tensor("wkb", [128, 8 * C], BF16))
        wvb = _E(nc.sbuf_tensor("wvb", [128, 8 * C], BF16))
        bqr = _E(nc.sbuf_tensor("bqr", [1, C], BF16))
        bkr = _E(nc.sbuf_tensor("bkr", [1, C], BF16))
        bvr = _E(nc.sbuf_tensor("bvr", [1, C], BF16))
        wd8 = _E(nc.sbuf_tensor("wd8", [128, 32 * C], FP8))
        gammaB = _E(nc.sbuf_tensor("gammaB", [128, C], F32))
        betaB = _E(nc.sbuf_tensor("betaB", [128, C], F32))
        kvS = _E(nc.sbuf_tensor("kvS", [128, 8 * 128], BF16))
        kmB = _E(nc.sbuf_tensor("kmB", [128, 16], BF16))
        ksacc = _E(nc.sbuf_tensor("ksacc", [128, C], F32))
        idn = _E(nc.sbuf_tensor("idn_s", [128, 128], BF16))
        idnf = _E(nc.sbuf_tensor("idnf_s", [128, 128], F32))
        ones1b = _E(nc.sbuf_tensor("ones1b_s", [1, 128], BF16))
        onesf = _E(nc.sbuf_tensor("onesf_s", [128, 1], F32))
        ones1f = _E(nc.sbuf_tensor("ones1f_s", [1, 128], F32))
        halfb = _E(nc.sbuf_tensor("halfb", [128, 1], F32))
        nhalfb = _E(nc.sbuf_tensor("nhalfb", [128, 1], F32))
        wjunk = _E(nc.sbuf_tensor("wjunk", [128, 1], F32))
        none1b = _E(nc.sbuf_tensor("none1b", [128, 1], F32))
        varp = _E(nc.sbuf_tensor("varp", [128, 2], F32))
        eps5b = _E(nc.sbuf_tensor("eps5b", [128, 1], F32))
        eps12b = _E(nc.sbuf_tensor("eps12b", [128, 1], F32))
        mwgb = _E(nc.sbuf_tensor("mwgb", [128, 1], F32))
        mwub = _E(nc.sbuf_tensor("mwub", [128, 1], F32))
        mwdb = _E(nc.sbuf_tensor("mwdb", [128, 1], F32))
        swgb = _E(nc.sbuf_tensor("swgb", [128, 1], F32))
        swub = _E(nc.sbuf_tensor("swub", [128, 1], F32))
        swdb = _E(nc.sbuf_tensor("swdb", [128, 1], F32))
        xb0 = _E(nc.sbuf_tensor("xb0", [128, C], F32))
        xb1 = _E(nc.sbuf_tensor("xb1", [128, C], F32))
        tb0 = _E(nc.sbuf_tensor("tb0", [128, C], F32))
        tb1 = _E(nc.sbuf_tensor("tb1", [128, C], F32))
        wp0 = xb0
        wp1 = xb1
        qc = _E(nc.sbuf_tensor("qc", [128, C], I32))
        lnb = _E(nc.sbuf_tensor("lnb", [128, C], BF16))
        lnT = _E(nc.sbuf_tensor("lnT", [128, C], BF16))
        sqscr = _E(nc.sbuf_tensor("sqscr", [128, C], F32))
        kg = _E(nc.sbuf_tensor("kg", [128, C], F32))
        rtmp = _E(nc.sbuf_tensor("rtmp", [128, C], F32))
        rt1 = _E(nc.sbuf_tensor("rt1", [128, C], F32))
        qt = rtmp
        qd = rt1
        ropeb = _E(nc.sbuf_tensor("ropeb", [128, C], BF16))
        vbt = _E(nc.sbuf_tensor("vbt", [128, C], BF16))
        qbt = _E(nc.sbuf_tensor("qbt", [128, C], BF16))
        qTt = _E(nc.sbuf_tensor("qTt", [128, C], BF16))
        qrT = _E(nc.sbuf_tensor("qrT", [128, C], BF16))
        ztt = _E(nc.sbuf_tensor("ztt", [128, 16], F32))
        attn = _E(nc.sbuf_tensor("attn", [128, C], F32))
        rest = _E(nc.sbuf_tensor("rest", [128, C], F32))
        x2t = _E(nc.sbuf_tensor("x2t", [128, C], F32))
        wgp0 = _E(nc.sbuf_tensor("wgp0", [128, INNER], FP8))
        wgp1 = _E(nc.sbuf_tensor("wgp1", [128, INNER], FP8))
        wup0 = _E(nc.sbuf_tensor("wup0", [128, INNER], FP8))
        wup1 = _E(nc.sbuf_tensor("wup1", [128, INNER], FP8))
        gtmp = _E(nc.sbuf_tensor("gtmp", [128, 512], F32))
        utmp = _E(nc.sbuf_tensor("utmp", [128, 512], F32))
        gu = _E(nc.sbuf_tensor("gu", [128, INNER], BF16))
        guqT = _E(nc.sbuf_tensor("guqT", [128, INNER], BF16))
        ssum = _E(nc.sbuf_tensor("ssum", [128, 2], F32))
        sqsum = _E(nc.sbuf_tensor("sqsum", [128, 2], F32))
        mm_ = _E(nc.sbuf_tensor("mm_", [128, 2], F32))
        msq = _E(nc.sbuf_tensor("msq", [128, 2], F32))
        varr = _E(nc.sbuf_tensor("varr", [128, 2], F32))
        rstd = _E(nc.sbuf_tensor("rstd", [128, 2], F32))
        amax = _E(nc.sbuf_tensor("amax", [128, 1], F32))
        amx2 = _E(nc.sbuf_tensor("amx2", [128, 1], F32))
        sa = _E(nc.sbuf_tensor("sa", [128, 1], F32))
        dsg = _E(nc.sbuf_tensor("dsg", [128, 1], F32))
        dsu = _E(nc.sbuf_tensor("dsu", [128, 1], F32))
        ddn = _E(nc.sbuf_tensor("ddn", [128, 1], F32))
        sabsg = _E(nc.sbuf_tensor("sabsg", [128, 1], F32))
        sabsu = _E(nc.sbuf_tensor("sabsu", [128, 1], F32))
        sabsd = _E(nc.sbuf_tensor("sabsd", [128, 1], F32))
        acc1 = _E(nc.sbuf_tensor("acc1", [128, 1], F32))
        row1 = _E(nc.sbuf_tensor("row1", [1, C], F32))
        tiny = _E(nc.sbuf_tensor("tiny", [1, 8], F32))
        P1 = _E(nc.psum_tensor("P1", [128, C], F32))
        P2 = _E(nc.psum_tensor("P2", [128, C], F32))
        P3 = _E(nc.psum_tensor("P3", [128, C], F32))
        P4 = _E(nc.psum_tensor("P4", [128, C], BF16))
        P5 = _E(nc.psum_tensor("P5", [128, 512], F32))
        _sL = _E(nc.semaphore("sL"))
        _sLG = _E(nc.semaphore("sLG"))
        _sPE = _E(nc.semaphore("sPE"))
        _sDVE = _E(nc.semaphore("sDVE"))
        _sACT = _E(nc.semaphore("sACT"))
        _sPool = _E(nc.semaphore("sPool"))
        _sX2 = _E(nc.semaphore("sX2"))
        _sXA = _E(nc.semaphore("sXA"))
        _sXB = _E(nc.semaphore("sXB"))
        _sTA = _E(nc.semaphore("sTA"))
        _sTB = _E(nc.semaphore("sTB"))
        _sG0 = _E(nc.semaphore("sG0"))
        _sG1 = _E(nc.semaphore("sG1"))
        _sU0 = _E(nc.semaphore("sU0"))
        _sU1 = _E(nc.semaphore("sU1"))
        _sRow = _E(nc.semaphore("sRow"))
        _sW8 = _E(nc.semaphore("sW8"))
        _sSt = _E(nc.semaphore("sSt"))
        p = Plan(nc)
        p.sems = dict(sL=_sL, sLG=_sLG, sPE=_sPE, sDVE=_sDVE, sACT=_sACT,
                      sPool=_sPool, sX2=_sX2, sW8=_sW8, sSt=_sSt,
                      sXA=_sXA, sXB=_sXB, sTA=_sTA, sTB=_sTB,
                      sG0=_sG0, sG1=_sG1, sU0=_sU0, sU1=_sU1, sRow=_sRow)
        p.cnt = {k: 0 for k in p.sems}

        inv_c = 1.0 / C
        state = {"p4_free": 0, "ln_read": 0, "ln_read_pool": 0, "q_free": 0, "ln_apply": 0}

        ln_gate = {0: {"pool": 0, "apply": 0}, 1: {"pool": 0, "apply": 0}}
        ln_i = [1]

        def ln_stats(x_tile):
            i = ln_i[0] = 1 - ln_i[0]
            s_ = ssum[:, i:i + 1]
            q_ = sqsum[:, i:i + 1]
            p.wait("scalar", "sPool", ln_gate[i]["pool"])
            p.emit("scalar", lambda e, s_=s_: e.activation(
                out=sqscr[:], in_=x_tile[:], func=AF.Identity, accum_out=s_))
            v = p.emit("scalar", lambda e, q_=q_: e.activation(
                out=sqscr[:], in_=x_tile[:], func=AF.Square, accum_out=q_),
                inc="sACT")
            return i, act_fence([s_, q_], v)

        def ln_smalls(sv, eps):
            i, act_val = sv
            s_ = ssum[:, i:i + 1]
            q_ = sqsum[:, i:i + 1]
            m_ = mm_[:, i:i + 1]
            x_ = msq[:, i:i + 1]
            v_ = varr[:, i:i + 1]
            r_ = rstd[:, i:i + 1]
            p.wait("gpsimd", "sACT", act_val)
            p.wait("gpsimd", "sDVE", ln_gate[i]["apply"])
            p.emit("gpsimd", lambda e, m_=m_, s_=s_: e.tensor_scalar(
                out=m_, in0=s_, scalar1=inv_c, scalar2=None, op0=ALU.mult))
            p.emit("gpsimd", lambda e, v_=v_, q_=q_, eps=eps: e.tensor_scalar(
                out=v_, in0=q_, scalar1=inv_c, scalar2=eps,
                op0=ALU.mult, op1=ALU.add))
            p.emit("gpsimd", lambda e, x_=x_, m_=m_: e.tensor_tensor(
                out=x_, in0=m_, in1=m_, op=ALU.mult))
            p.emit("gpsimd", lambda e, v_=v_, x_=x_: e.tensor_tensor(
                out=v_, in0=v_, in1=x_, op=ALU.subtract))
            pv = p.emit("gpsimd", lambda e, r_=r_, v_=v_: e.tensor_tensor(
                out=r_, in0=v_, in1=nhalfb[:], op=ALU.pow), inc="sPool")
            ln_gate[i]["pool"] = pv
            p.wait("gpsimd", "sPool", pv)
            pv = p.emit("gpsimd", lambda e, r_=r_: e.tensor_scalar(
                out=r_, in0=r_, scalar1=1.0, scalar2=None, op0=ALU.mult))
            pv = p.emit("gpsimd", lambda e, m_=m_: e.tensor_scalar(
                out=m_, in0=m_, scalar1=1.0, scalar2=None, op0=ALU.mult),
                inc="sPool")
            ln_gate[i]["pool"] = pv
            return (i, pv)

        def ln_apply(pv, x_tile, out_tile):
            i, pool_val = pv
            m_ = mm_[:, i:i + 1]
            r_ = rstd[:, i:i + 1]
            p.wait("vector", "sPool", pool_val)
            v = p.emit("vector", lambda e, m_=m_, r_=r_: e.tensor_scalar(
                out=out_tile[:], in0=x_tile[:], scalar1=m_, scalar2=r_,
                op0=ALU.subtract, op1=ALU.mult), inc="sDVE")
            ln_gate[i]["apply"] = v
            return v

        def dve_fence(tiles, v0):
            p.wait("vector", "sDVE", v0)
            p.emit("vector", lambda e: e.engine_nop())
            p.emit("vector", lambda e: e.engine_nop())
            v = None
            for i, tl in enumerate(tiles):
                v = p.emit("vector", lambda e, tl=tl: e.tensor_scalar(
                    out=tl, in0=tl, scalar1=1.0, scalar2=None, op0=ALU.mult),
                    inc="sDVE" if i == len(tiles) - 1 else None)
            return v

        def act_fence(tiles, v0):
            p.wait("scalar", "sACT", v0)
            p.emit("scalar", lambda e: e.activation(
                out=wjunk[:], in_=halfb[:], func=AF.Identity))
            p.emit("scalar", lambda e: e.activation(
                out=wjunk[:], in_=halfb[:], func=AF.Identity))
            v = None
            for i, tl in enumerate(tiles):
                v = p.emit("scalar", lambda e, tl=tl: e.activation(
                    out=tl, in_=tl, func=AF.Identity),
                    inc="sACT" if i == len(tiles) - 1 else None)
            return v

        def transpose_pack(src_bf16, dst_bf16, src_off=0, dst_off=0):
            p.wait("tensor", "sDVE", state["p4_free"])
            for i in range(8):
                v = p.emit("tensor", lambda e, i=i: e.transpose(
                    out=P4[:, 128 * i:128 * i + 128],
                    in_=src_bf16[:, src_off + 128 * i:src_off + 128 * i + 128],
                    identity=idn[:]),
                    inc="sPE" if i == 7 else None)
            p.wait("vector", "sPE", v)
            dv = p.emit("vector", lambda e: e.tensor_copy(
                dst_bf16[:, dst_off:dst_off + C], P4[:]), inc="sDVE")
            state["p4_free"] = dv
            return dv

        def proj(wb, brow, psum, lnT_val):
            p.wait("tensor", "sDVE", lnT_val)
            pe_val = None
            for h in range(2):
                for c in range(8):
                    p.emit("tensor", lambda e, h=h, c=c, wb=wb: e.matmul(
                        out=psum[:, 512 * h:512 * h + 512],
                        lhsT=lnT[:, 128 * c:128 * c + 128],
                        rhs=wb[:, C * c + 512 * h:C * c + 512 * h + 512],
                        start=(c == 0), stop=False))
                pe_val = p.emit("tensor", lambda e, h=h, brow=brow: e.matmul(
                    out=psum[:, 512 * h:512 * h + 512],
                    lhsT=ones1b[:],
                    rhs=brow[:, 512 * h:512 * h + 512],
                    start=False, stop=True), inc="sPE")
            return pe_val

        def rope(src_f32, out_bf16, tab):
            cosv = tab[:, 0:512].rearrange("p (a b) -> p a b", b=1).broadcast_to(
                [128, 512, 2])
            sinv = tab[:, 512:1024].rearrange("p (a b) -> p a b", b=1).broadcast_to(
                [128, 512, 2])
            p.emit("vector", lambda e: e.tensor_scalar(
                out=rtmp[:, 0:C:2], in0=src_f32[:, 1:C:2], scalar1=-1.0,
                scalar2=-0.21, op0=ALU.mult, op1=ALU.add))
            p.emit("vector", lambda e: e.tensor_scalar(
                out=rtmp[:, 1:C:2], in0=src_f32[:, 0:C:2], scalar1=0.21,
                scalar2=None, op0=ALU.add))
            p.emit("vector", lambda e, cosv=cosv: e.scalar_tensor_tensor(
                out=rt1[:].rearrange("p (a b) -> p a b", b=2),
                in0=src_f32[:].rearrange("p (a b) -> p a b", b=2),
                scalar=0.21, in1=cosv, op0=ALU.add, op1=ALU.mult))
            p.emit("vector", lambda e, sinv=sinv: e.tensor_tensor(
                out=rtmp[:].rearrange("p (a b) -> p a b", b=2),
                in0=rtmp[:].rearrange("p (a b) -> p a b", b=2),
                in1=sinv, op=ALU.mult))
            return p.emit("vector", lambda e: e.tensor_tensor(
                out=out_bf16[:], in0=rt1[:], in1=rtmp[:], op=ALU.add),
                inc="sDVE")

        def quant_round(src_ap, dst_ap, scale_ap, width, scale_val=None):
            p.wait("scalar", "sDVE", state["q_free"])
            if scale_val is not None:
                p.wait("scalar", "sPool", scale_val)
            av = p.emit("scalar", lambda e: e.activation(
                out=qt[:, :width], in_=src_ap, func=AF.Identity,
                scale=scale_ap[:], bias=halfb[:]), inc="sACT")
            p.wait("vector", "sACT", av)
            p.emit("vector", lambda e: e.tensor_copy(qc[:, :width], qt[:, :width]))
            p.emit("vector", lambda e: e.tensor_tensor(
                out=qd[:, :width], in0=qt[:, :width], in1=qc[:, :width],
                op=ALU.is_lt))
            v = p.emit("vector", lambda e: e.tensor_tensor(
                out=dst_ap, in0=qc[:, :width], in1=qd[:, :width],
                op=ALU.subtract), inc="sDVE")
            state["q_free"] = v
            return v

        def dbg_tap(name, tile, eng_sem, val):
            if not debug or name not in dbg:
                return
            p.wait("gpsimd", eng_sem, val)
            p.emit("gpsimd", lambda e, name=name, tile=tile: e.dma_start(
                out=dbg[name], in_=tile[:]), inc="sSt", amt=16)

        def dbg_barrier():
            if not debug:
                return
            v = p.now("sSt")
            for e_ in ENGINES:
                p.wait(e_, "sSt", v)

        def tile_barrier():
            if not barriers:
                return
            for s_ in p.sems:
                v = p.now(s_)
                for e_ in ENGINES:
                    p.wait(e_, s_, v)

        p.emit("gpsimd", lambda e: e.memset(halfb[:], 0.5))
        p.emit("gpsimd", lambda e: e.memset(nhalfb[:], -0.5))
        p.emit("gpsimd", lambda e: e.memset(none1b[:], -1.0))
        p.emit("gpsimd", lambda e: e.memset(eps5b[:], 1e-5))
        p.emit("gpsimd", lambda e: e.memset(eps12b[:], 1e-12))
        p.emit("gpsimd", lambda e: e.memset(kvS[:], 0.0))
        p.emit("gpsimd", lambda e: e.memset(kmB[:], 0.0))
        p.emit("gpsimd", lambda e: e.memset(sabsg[:], 0.0))
        p.emit("gpsimd", lambda e: e.memset(sabsu[:], 0.0))
        p.emit("gpsimd", lambda e: e.memset(sabsd[:], 0.0))
        mz = p.emit("gpsimd", lambda e: e.memset(ksacc[:], 0.0), inc="sPool")
        p.wait("scalar", "sPool", mz)
        p.emit("scalar", lambda e: e.activation(
            out=wjunk[:], in_=halfb[:], func=AF.Abs))

        for t_, s_ in [(idn, idn_d), (idnf, idnf_d), (ones1b, ones1b_d),
                       (onesf, onesf_d), (ones1f, ones1f_d)]:
            p.emit("sync", lambda e, t_=t_, s_=s_: e.dma_start(out=t_[:], in_=s_),
                   inc="sL", amt=16)
        base_l = p.now("sL")

        for src_d, dst in [(gamma_d, gammaB), (beta_d, betaB)]:
            lv = p.emit("sync", lambda e, src_d=src_d: e.dma_start(
                out=row1[:], in_=src_d), inc="sRow", amt=16)
            p.wait("tensor", "sRow", lv)
            p.wait("tensor", "sL", base_l)
            for h in range(2):
                v = p.emit("tensor", lambda e, h=h: e.matmul(
                    out=P5[:], lhsT=ones1f[:],
                    rhs=row1[:, 512 * h:512 * h + 512],
                    start=True, stop=True), inc="sPE")
                p.wait("scalar", "sPE", v)
                av = p.emit("scalar", lambda e, h=h, dst=dst: e.copy(
                    dst[:, 512 * h:512 * h + 512], P5[:]), inc="sACT")
                p.wait("tensor", "sACT", av)
            p.wait("sync", "sACT", p.now("sACT"))

        cast_acts = []
        for wi, (wdram, wtile) in enumerate([(wk_d, wkb), (wv_d, wvb),
                                             (wq_d, wqb)]):
            for c in range(8):
                i = len(cast_acts)
                buf = wp0 if i % 2 == 0 else wp1
                if i >= 2:
                    p.wait("sync", "sACT", cast_acts[i - 2])
                bsem = "sXA" if i % 2 == 0 else "sXB"
                lv = p.emit("sync", lambda e, wdram=wdram, c=c, buf=buf:
                            e.dma_start(out=buf[:],
                                        in_=wdram[128 * c:128 * c + 128, :]),
                            inc=bsem, amt=16)
                p.wait("scalar", bsem, lv)
                av = p.emit("scalar", lambda e, wtile=wtile, c=c, buf=buf:
                            e.copy(wtile[:, C * c:C * c + C], buf[:]),
                            inc="sACT")
                cast_acts.append(av)
        for wdram, btile in [(wk_d, bkr), (wv_d, bvr), (wq_d, bqr)]:
            p.wait("sync", "sACT", p.now("sACT"))
            lv = p.emit("sync", lambda e, wdram=wdram: e.dma_start(
                out=row1[:], in_=wdram[1024:1025, :]), inc="sRow", amt=16)
            p.wait("scalar", "sRow", lv)
            p.emit("scalar", lambda e, btile=btile: e.copy(btile[:], row1[:]),
                   inc="sACT")
        qkv_ready = p.now("sACT")

        def wchunks(wdram, rows, cols):
            out = []
            for r in range(rows // 128):
                for j in range(cols // 1024):
                    out.append((wdram[128 * r:128 * r + 128,
                                      1024 * j:1024 * j + 1024], r, j))
            return out

        pool0 = p.now("sPool")
        skip_ffn = "D" not in phases
        abs_jobs = [] if skip_ffn else ([(a_, sabsg) for a_, r, j in wchunks(wg_d, C, INNER)]
                    + [(a_, sabsu) for a_, r, j in wchunks(wu_d, C, INNER)]
                    + [(a_, sabsd) for a_, r, j in wchunks(wd_d, INNER, C)])
        p.wait("gpsimd", "sACT", qkv_ready)
        abs_acts = []
        abs_dves = []
        for i, (src_ap, acc_t) in enumerate(abs_jobs):
            buf = wp0 if i % 2 == 0 else wp1
            if i >= 2:
                p.wait("gpsimd", "sACT", abs_acts[i - 2])
            bsem = "sXA" if i % 2 == 0 else "sXB"
            lv = p.emit("gpsimd", lambda e, s=src_ap, b=buf: e.dma_start(
                out=b[:], in_=s), inc=bsem, amt=16)
            p.wait("scalar", bsem, lv)
            p.wait("scalar", "sPool", pool0)
            av = p.emit("scalar", lambda e, b=buf: e.activation(
                out=sqscr[:], in_=b[:], func=AF.Abs, accum_out=acc1[:]),
                inc="sACT")
            abs_acts.append(av)
            p.wait("vector", "sACT", av)
            dv = p.emit("vector", lambda e, acc_t=acc_t: e.tensor_tensor(
                out=acc_t[:], in0=acc_t[:], in1=acc1[:], op=ALU.add),
                inc="sDVE")
            abs_dves.append(dv)

        _fv = dve_fence([sabsg[:], sabsu[:], sabsd[:]], p.now("sDVE"))
        p.wait("tensor", "sDVE", _fv)
        p.wait("tensor", "sL", base_l)
        for acc_t, mwt, swt in ([] if skip_ffn else
                               [(sabsg, mwgb, swgb), (sabsu, mwub, swub),
                                (sabsd, mwdb, swdb)]):
            v = p.emit("tensor", lambda e, acc_t=acc_t: e.matmul(
                out=P5[0:1, 0:1], lhsT=acc_t[:], rhs=onesf[:],
                start=True, stop=True), inc="sPE")
            p.wait("vector", "sPE", v)
            dv = p.emit("vector", lambda e: e.tensor_copy(
                tiny[:, 2:3], P5[0:1, 0:1].broadcast_to([1, 1])), inc="sDVE")
            p.wait("gpsimd", "sDVE", dv)
            p.emit("gpsimd", lambda e: e.tensor_scalar(
                out=tiny[:, 0:1], in0=tiny[:, 2:3],
                scalar1=1.0 / (C * INNER), scalar2=1e-5,
                op0=ALU.mult, op1=ALU.max))
            p.emit("gpsimd", lambda e: e.tensor_tensor(
                out=tiny[:, 1:2], in0=tiny[:, 0:1], in1=none1b[0:1, 0:1],
                op=ALU.pow), inc="sPool")
            pv0 = p.now("sPool")
            p.wait("gpsimd", "sPool", pv0)
            dv = p.emit("gpsimd", lambda e: e.tensor_scalar(
                out=tiny[:, 0:2], in0=tiny[:, 0:2], scalar1=1.0, scalar2=None,
                op0=ALU.mult), inc="sPool")
            p.wait("tensor", "sPool", dv)
            p.emit("tensor", lambda e: e.matmul(
                out=P5[:, 0:1], lhsT=ones1f[:], rhs=tiny[:, 0:1],
                start=True, stop=True))
            v = p.emit("tensor", lambda e: e.matmul(
                out=P5[:, 1:2], lhsT=ones1f[:], rhs=tiny[:, 1:2],
                start=True, stop=True), inc="sPE")
            p.wait("scalar", "sPE", v)
            p.emit("scalar", lambda e, mwt=mwt: e.copy(mwt[:], P5[:, 0:1]))
            av = p.emit("scalar", lambda e, swt=swt: e.copy(swt[:], P5[:, 1:2]),
                        inc="sACT")
            p.wait("tensor", "sACT", av)

        q_jobs = [] if skip_ffn else (
                  [(a_, swgb, "g", r, j) for a_, r, j in wchunks(wg_d, C, INNER)]
                  + [(a_, swub, "u", r, j) for a_, r, j in wchunks(wu_d, C, INNER)]
                  + [(a_, swdb, "d", r, j) for a_, r, j in wchunks(wd_d, INNER, C)])
        q_done = []
        q_stores = []
        p.wait("gpsimd", "sACT", p.now("sACT"))
        for i, (src_ap, sw_t, kind, r, j) in enumerate(q_jobs):
            buf = wp0 if i % 2 == 0 else wp1
            if i >= 2:
                p.wait("gpsimd", "sACT", q_tp[i - 2])
            bsem = "sXA" if i % 2 == 0 else "sXB"
            lv = p.emit("gpsimd", lambda e, s=src_ap, b=buf: e.dma_start(
                out=b[:], in_=s), inc=bsem, amt=16)
            p.wait("scalar", bsem, lv)
            p.wait("scalar", "sDVE", state["q_free"])
            av = p.emit("scalar", lambda e, b=buf, sw_t=sw_t: e.activation(
                out=qt[:], in_=b[:], func=AF.Identity, scale=sw_t[:],
                bias=halfb[:]), inc="sACT")
            if i == 0:
                q_tp = [av]
            else:
                q_tp.append(av)
            p.wait("vector", "sACT", av)
            p.emit("vector", lambda e: e.tensor_copy(qc[:], qt[:]))
            p.emit("vector", lambda e: e.tensor_tensor(
                out=qd[:], in0=qt[:], in1=qc[:], op=ALU.is_lt))
            p.emit("vector", lambda e: e.tensor_tensor(
                out=qt[:], in0=qc[:], in1=qd[:], op=ALU.subtract))
            if kind == "d":
                v = p.emit("vector", lambda e, r=r: e.tensor_scalar(
                    out=wd8[:, C * r:C * r + C], in0=qt[:], scalar1=1.0,
                    scalar2=-1.0, op0=ALU.min, op1=ALU.max), inc="sDVE")
                state["q_free"] = v
                q_done.append(v)
                q_stores.append(None)
            else:
                slot = i % 4
                if i >= 4 and q_stores[i - 4] is not None:
                    p.wait("vector", "sW8", p.now("sW8"))
                v = p.emit("vector", lambda e, slot=slot: e.tensor_scalar(
                    out=wgp1[:, C * slot:C * slot + C], in0=qt[:], scalar1=1.0,
                    scalar2=-1.0, op0=ALU.min, op1=ALU.max), inc="sDVE")
                state["q_free"] = v
                q_done.append(v)
                w8d = wg8_d if kind == "g" else wu8_d
                p.wait("sync", "sDVE", v)
                sv = None
                for pp in range(2):
                    pc = 2 * j + pp
                    sv = p.emit("sync",
                                lambda e, w8d=w8d, pc=pc, r=r, slot=slot, pp=pp:
                                e.dma_start(
                                    out=w8d[pc, r],
                                    in_=wgp1[:, C * slot + 512 * pp:
                                            C * slot + 512 * pp + 512]),
                                inc="sW8", amt=16)
                q_stores.append(sv)
        wd8_ready = p.now("sDVE")
        w8_all = p.now("sW8")

        p.emit("scalar", lambda e: e.activation(
            out=wjunk[:], in_=halfb[:], func=AF.Gelu))
        p.wait("tensor", "sACT", qkv_ready)
        p.wait("vector", "sPool", mz)
        p.wait("sync", "sACT", p.now("sACT"))
        p.wait("sync", "sDVE", p.now("sDVE"))
        a_free = []
        kproj_evac = 0
        vproj_evac = 0
        kv_started = False
        for t in range(n_tiles):
            if t == dbg_t + 1:
                dbg_barrier()
            tile_barrier()
            xb = xb0 if t % 2 == 0 else xb1
            tb = tb0 if t % 2 == 0 else tb1
            if t >= 2:
                p.wait("sync", "sDVE", a_free[t - 2])
            xsem = "sXA" if t % 2 == 0 else "sXB"
            tsem = "sTA" if t % 2 == 0 else "sTB"
            lx = p.emit("sync", lambda e, t=t, xb=xb: e.dma_start(
                out=xb[:], in_=x_d[128 * t:128 * t + 128, :]), inc=xsem, amt=16)
            lt = p.emit("sync", lambda e, t=t, tb=tb: e.dma_start(
                out=tb[:], in_=tabs_d[128 * t:128 * t + 128, :]), inc=tsem,
                amt=16)
            p.wait("scalar", xsem, lx)
            sv = ln_stats(xb)
            pvv = ln_smalls(sv, 1e-5)
            lnb_v = ln_apply(pvv, xb, lnb)
            p.wait("tensor", "sDVE", lnb_v)
            lnT_v = transpose_pack(lnb, lnT)
            if t > 0:
                p.wait("tensor", "sACT", kproj_evac)
            kp = proj(wkb, bkr, P1, lnT_v)
            p.wait("scalar", "sPE", kp)
            kproj_evac = p.emit("scalar", lambda e: e.activation(
                out=kg[:], in_=P1[:], func=AF.Gelu), inc="sACT")
            if t > 0:
                p.wait("tensor", "sACT", vproj_evac)
            vp = proj(wvb, bvr, P2, lnT_v)
            p.wait("scalar", "sPE", vp)
            vproj_evac = p.emit("scalar", lambda e: e.activation(
                out=vbt[:], in_=P2[:], func=AF.Identity, scale=1.0 / N),
                inc="sACT")
            p.wait("vector", "sACT", kproj_evac)
            p.wait("vector", tsem, lt)
            if t > 0:
                p.wait("vector", "sPE", kv_pe_prev)
            rope_v = rope(kg, ropeb, tb)
            ks_v = p.emit("vector", lambda e: e.scalar_tensor_tensor(
                out=ksacc[:], in0=kg[:], scalar=0.21, in1=ksacc[:],
                op0=ALU.add, op1=ALU.add), inc="sDVE")
            a_free.append(ks_v)
            p.wait("tensor", "sDVE", rope_v)
            p.wait("tensor", "sACT", vproj_evac)
            for h in range(HEADS):
                first = (not kv_started) and (h == 0 or h == 8)
                last = (t == n_tiles - 1) and (h == 7 or h == 15)
                p.emit("tensor", lambda e, h=h, first=first, last=last:
                       e.matmul(out=P3[0:64, 64 * h:64 * h + 64],
                                lhsT=ropeb[:, 64 * h:64 * h + 64],
                                rhs=vbt[:, 64 * h:64 * h + 64],
                                start=first, stop=last),
                       inc="sPE" if h == HEADS - 1 else None)
            kv_pe_prev = p.now("sPE")
            kv_started = True
            if debug and t == dbg_t:
                dbg_tap("a_x", xb, xsem, lx)
                p.wait("gpsimd", "sDVE", lnb_v)
                p.emit("gpsimd", lambda e: e.tensor_copy(ztt[:, 0:1], ssum[:, 0:1]))
                p.emit("gpsimd", lambda e: e.tensor_copy(ztt[:, 1:2], sqsum[:, 0:1]))
                p.emit("gpsimd", lambda e: e.tensor_copy(ztt[:, 2:3], mm_[:, 0:1]))
                p.emit("gpsimd", lambda e: e.tensor_copy(ztt[:, 3:4], rstd[:, 0:1]))
                p.emit("gpsimd", lambda e: e.tensor_copy(ztt[:, 4:5], varp[:, 0:1]))
                sdbg = p.emit("gpsimd", lambda e: e.tensor_copy(ztt[:, 5:6], varr[:, 0:1]), inc="sPool")
                dbg_tap("a_stats", ztt, "sPool", sdbg)
                dbg_tap("a_lnb", lnb, "sDVE", lnb_v)
                dbg_tap("a_kg", kg, "sACT", kproj_evac)
                dbg_tap("a_ropeb", ropeb, "sDVE", rope_v)
                dbg_tap("a_vb", vbt, "sACT", vproj_evac)
        kv_done = p.now("sPE")

        p.wait("scalar", "sPE", kv_done)
        for c in range(8):
            h0, h1 = 2 * c, 2 * c + 1
            p.emit("scalar", lambda e, c=c, h0=h0: e.copy(
                kvS[0:64, 128 * c:128 * c + 64],
                P3[0:64, 64 * h0:64 * h0 + 64]))
            p.emit("scalar", lambda e, c=c, h1=h1: e.copy(
                kvS[64:128, 128 * c + 64:128 * c + 128],
                P3[0:64, 64 * h1:64 * h1 + 64]),
                inc="sACT" if c == 7 else None)
        kvS_ready = act_fence([kvS[:]], p.now("sACT"))

        _fv = dve_fence([ksacc[:]], p.now("sDVE"))
        p.wait("tensor", "sDVE", _fv)
        for h in range(2):
            v = p.emit("tensor", lambda e, h=h: e.matmul(
                out=P5[0:1, :], lhsT=onesf[:],
                rhs=ksacc[:, 512 * h:512 * h + 512],
                start=True, stop=True), inc="sPE")
            p.wait("scalar", "sPE", v)
            av = p.emit("scalar", lambda e, h=h: e.activation(
                out=row1[:, 512 * h:512 * h + 512], in_=P5[0:1, :],
                func=AF.Identity, scale=1.0 / N), inc="sACT")
            p.wait("tensor", "sACT", av)
        _fv = act_fence([row1[:]], p.now("sACT"))
        p.wait("tensor", "sACT", _fv)
        for c in range(8):
            v = p.emit("tensor", lambda e, c=c: e.transpose(
                out=P5[:, 0:1], in_=row1[0:1, 128 * c:128 * c + 128],
                identity=ones1f[0:1, 0:1]), inc="sPE")
            p.wait("scalar", "sPE", v)
            p.emit("scalar", lambda e, c=c: e.copy(
                kmB[0:64, 2 * c:2 * c + 1], P5[0:64, 0:1]))
            av = p.emit("scalar", lambda e, c=c: e.copy(
                kmB[64:128, 2 * c + 1:2 * c + 2], P5[64:128, 0:1]),
                inc="sACT")
            p.wait("tensor", "sACT", av)
        kmB_ready = act_fence([kmB[:]], p.now("sACT"))
        if debug:
            dbg_tap("kvS", kvS, "sACT", kvS_ready)
            dbg_tap("kmB", kmB, "sACT", kmB_ready)

        if "B" not in phases:
            n_tiles_b = 0
        else:
            n_tiles_b = n_tiles
        b_free = []
        qproj_evac = 0
        z_evac = 0
        o_evac = 0
        for t in range(n_tiles_b):
            if t == dbg_t + 1:
                dbg_barrier()
            tile_barrier()
            xb = xb0 if t % 2 == 0 else xb1
            tb = tb0 if t % 2 == 0 else tb1
            if t >= 2:
                p.wait("sync", "sDVE", b_free[t - 2])
            elif t == 0:
                p.wait("sync", "sACT", kvS_ready)
            xsem = "sXA" if t % 2 == 0 else "sXB"
            tsem = "sTA" if t % 2 == 0 else "sTB"
            lx = p.emit("sync", lambda e, t=t, xb=xb: e.dma_start(
                out=xb[:], in_=x_d[128 * t:128 * t + 128, :]), inc=xsem, amt=16)
            lt = p.emit("sync", lambda e, t=t, tb=tb: e.dma_start(
                out=tb[:], in_=tabs_d[128 * t:128 * t + 128, :]), inc=tsem,
                amt=16)
            p.wait("scalar", xsem, lx)
            sv = ln_stats(xb)
            pvv = ln_smalls(sv, 1e-5)
            lnb_v = ln_apply(pvv, xb, lnb)
            p.wait("tensor", "sDVE", lnb_v)
            lnT_v = transpose_pack(lnb, lnT)
            if t > 0:
                p.wait("tensor", "sACT", qproj_evac)
            qp = proj(wqb, bqr, P1, lnT_v)
            p.wait("scalar", "sPE", qp)
            qproj_evac = p.emit("scalar", lambda e: e.activation(
                out=kg[:], in_=P1[:], func=AF.Gelu), inc="sACT")
            p.wait("vector", "sACT", qproj_evac)
            qb_v = p.emit("vector", lambda e: e.tensor_scalar(
                out=qbt[:], in0=kg[:], scalar1=0.21, scalar2=None,
                op0=ALU.add), inc="sDVE")
            p.wait("vector", tsem, lt)
            rope_v = rope(kg, ropeb, tb)
            p.wait("tensor", "sDVE", qb_v)
            qT_v = transpose_pack(qbt, qTt)
            p.wait("tensor", "sDVE", rope_v)
            qrT_v = transpose_pack(ropeb, qrT)
            if t == 0:
                p.wait("tensor", "sACT", kmB_ready)
            else:
                p.wait("tensor", "sDVE", z_evac)
            for c in range(8):
                v = p.emit("tensor", lambda e, c=c: e.matmul(
                    out=P3[:, 2 * c:2 * c + 2],
                    lhsT=qTt[:, 128 * c:128 * c + 128],
                    rhs=kmB[:, 2 * c:2 * c + 2],
                    start=(c == 0), stop=(c == 7)),
                    inc="sPE" if c == 7 else None)
            p.wait("vector", "sPE", v)
            zpre_v = p.emit("vector", lambda e: e.tensor_scalar(
                out=ztt[:], in0=P3[:, 0:16], scalar1=1e-6, scalar2=None,
                op0=ALU.add), inc="sDVE")
            if debug and t == dbg_t:
                dbg_tap("b_zpre", ztt, "sDVE", zpre_v)
            p.wait("vector", "sDVE", zpre_v)
            p.emit("vector", lambda e: e.engine_nop())
            p.emit("vector", lambda e: e.engine_nop())
            z_evac = p.emit("vector", lambda e: e.reciprocal(ztt[:], ztt[:]),
                            inc="sDVE")
            p.wait("vector", "sDVE", z_evac)
            p.emit("vector", lambda e: e.engine_nop())
            p.emit("vector", lambda e: e.engine_nop())
            if t == 0:
                p.wait("tensor", "sACT", kvS_ready)
            else:
                p.wait("tensor", "sDVE", o_evac)
            p.wait("tensor", "sDVE", qrT_v)
            for c in range(8):
                v = p.emit("tensor", lambda e, c=c: e.matmul(
                    out=P2[:, 128 * c:128 * c + 128],
                    lhsT=qrT[:, 128 * c:128 * c + 128],
                    rhs=kvS[:, 128 * c:128 * c + 128],
                    start=(c == 0 or c == 4), stop=(c == 3 or c == 7)),
                    inc="sPE" if c == 7 else None)
            p.wait("vector", "sPE", v)
            o_evac = p.emit("vector", lambda e: e.tensor_tensor(
                out=attn[:].rearrange("p (h d) -> p h d", d=64),
                in0=P2[:].rearrange("p (h d) -> p h d", d=64),
                in1=ztt[:].rearrange("p (h d) -> p h d", d=1).broadcast_to(
                    [128, 16, 64]),
                op=ALU.mult), inc="sDVE")
            p.wait("gpsimd", "sDVE", o_evac)
            res_v = p.emit("gpsimd", lambda e, xb=xb: e.tensor_tensor(
                out=rest[:], in0=attn[:], in1=xb[:], op=ALU.add), inc="sPool")
            p.wait("scalar", "sPool", res_v)
            sv = ln_stats(rest)
            pvv = ln_smalls(sv, 1e-12)
            t1_v = ln_apply(pvv, rest, attn)
            p.emit("vector", lambda e: e.tensor_tensor(
                out=attn[:], in0=attn[:], in1=gammaB[:], op=ALU.mult))
            t2_v = p.emit("vector", lambda e: e.tensor_tensor(
                out=rest[:], in0=attn[:], in1=betaB[:], op=ALU.add),
                inc="sDVE")
            p.wait("scalar", "sDVE", t2_v)
            sv = ln_stats(rest)
            pvv = ln_smalls(sv, 1e-5)
            t3_v = ln_apply(pvv, rest, attn)
            if t > 0:
                p.wait("vector", "sX2", p.now("sX2"))
            x2_v = p.emit("vector", lambda e, xb=xb: e.tensor_tensor(
                out=x2t[:], in0=attn[:], in1=xb[:], op=ALU.add), inc="sDVE")
            b_free.append(x2_v)
            p.wait("sync", "sDVE", x2_v)
            p.emit("sync", lambda e, t=t: e.dma_start(
                out=x2_d[128 * t:128 * t + 128, :], in_=x2t[:]), inc="sX2",
                amt=16)
            if debug and t == dbg_t:
                dbg_tap("b_qb", qbt, "sDVE", qb_v)
                dbg_tap("b_zt", ztt, "sDVE", z_evac)
                dbg_tap("b_attn", attn, "sDVE", o_evac)
                dbg_tap("b_res", rest, "sPool", res_v)
                dbg_tap("b_x2", x2t, "sDVE", x2_v)
        x2_all = p.now("sX2")

        n_tiles_d = n_tiles if "D" in phases else 0
        p.emit("scalar", lambda e: e.activation(
            out=wjunk[:], in_=halfb[:], func=AF.Silu))
        p.wait("sync", "sX2", x2_all)
        p.wait("tensor", "sDVE", wd8_ready)
        p.wait("gpsimd", "sW8", w8_all)
        d_free = []
        dn_evac = {0: 0, 1: 0}
        gu_evac = {0: 0, 1: 0}
        piece_pe = {0: 0, 1: 0}
        xq_used = 0
        for t in range(n_tiles_d):
            if t == dbg_t + 1:
                dbg_barrier()
            tile_barrier()
            xb = xb0 if t % 2 == 0 else xb1
            if t >= 2:
                p.wait("sync", "sDVE", d_free[t - 2])
            xsem = "sXA" if t % 2 == 0 else "sXB"
            lx = p.emit("sync", lambda e, t=t, xb=xb: e.dma_start(
                out=xb[:], in_=x2_d[128 * t:128 * t + 128, :]), inc=xsem,
                amt=16)
            p.wait("scalar", xsem, lx)
            sv = ln_stats(xb)
            pvv = ln_smalls(sv, 1e-5)
            ln3_v = ln_apply(pvv, xb, kg)
            am_v = p.emit("vector", lambda e: e.reduce_max(
                out=amax[:], in_=kg[:], axis=AX.X,
                apply_absolute_value=True), inc="sDVE")
            am_v = dve_fence([amax[:]], am_v)
            p.wait("gpsimd", "sDVE", am_v)
            p.emit("gpsimd", lambda e: e.tensor_single_scalar(
                out=amax[:], in_=amax[:], scalar=1e-5, op=ALU.max))
            p.emit("gpsimd", lambda e: e.tensor_tensor(
                out=sa[:], in0=amax[:], in1=none1b[:], op=ALU.pow))
            p.emit("gpsimd", lambda e: e.tensor_scalar(
                out=sa[:], in0=sa[:], scalar1=127.0, scalar2=None,
                op0=ALU.mult))
            p.emit("gpsimd", lambda e: e.tensor_scalar(
                out=dsg[:], in0=amax[:], scalar1=mwgb[:], scalar2=1.0 / 127.0,
                op0=ALU.mult, op1=ALU.mult))
            sm_v = p.emit("gpsimd", lambda e: e.tensor_scalar(
                out=dsu[:], in0=amax[:], scalar1=mwub[:], scalar2=1.0 / 127.0,
                op0=ALU.mult, op1=ALU.mult), inc="sPool")
            p.wait("gpsimd", "sPool", sm_v)
            sm_v = p.emit("gpsimd", lambda e: e.tensor_scalar(
                out=sa[:], in0=sa[:], scalar1=1.0, scalar2=None,
                op0=ALU.mult), inc="sPool")
            sm_v = p.emit("gpsimd", lambda e: e.tensor_scalar(
                out=dsg[:], in0=dsg[:], scalar1=1.0, scalar2=None,
                op0=ALU.mult), inc="sPool")
            sm_v = p.emit("gpsimd", lambda e: e.tensor_scalar(
                out=dsu[:], in0=dsu[:], scalar1=1.0, scalar2=None,
                op0=ALU.mult), inc="sPool")
            xq_v = quant_round(kg[:], lnb[:], sa, C, scale_val=sm_v)
            p.wait("tensor", "sDVE", xq_v)
            xqT_v = transpose_pack(lnb, qTt)
            if debug and t == dbg_t:
                dbg_tap("d_ln3", kg, "sDVE", ln3_v)
                dbg_tap("d_xq", lnb, "sDVE", xq_v)
            gu_dves = []
            for pp in range(8):
                pb = pp % 2
                wg_buf = wgp0 if pb == 0 else wgp1
                wu_buf = wup0 if pb == 0 else wup1
                gsem = "sG0" if pb == 0 else "sG1"
                usem = "sU0" if pb == 0 else "sU1"
                p.wait("scalar", "sPE", piece_pe[pb])
                lg = p.emit("scalar", lambda e, pp=pp, b=wg_buf: e.dma_start(
                    out=b[:].rearrange("p (c n) -> p c n", c=8),
                    in_=wg8_d[pp].rearrange("c p n -> p c n")), inc=gsem,
                    amt=16)
                lw = p.emit("scalar", lambda e, pp=pp, b=wu_buf: e.dma_start(
                    out=b[:].rearrange("p (c n) -> p c n", c=8),
                    in_=wu8_d[pp].rearrange("c p n -> p c n")), inc=usem,
                    amt=16)
                p.wait("tensor", gsem, lg)
                p.wait("tensor", usem, lw)
                p.wait("tensor", "sDVE", xqT_v)
                p.wait("tensor", "sACT", gu_evac[pb])
                for c in range(8):
                    p.emit("tensor", lambda e, c=c, pb=pb, b=wg_buf: e.matmul(
                        out=P1[:, 512 * pb:512 * pb + 512],
                        lhsT=qTt[:, 128 * c:128 * c + 128],
                        rhs=b[:, 512 * c:512 * c + 512],
                        start=(c == 0), stop=(c == 7)))
                for c in range(8):
                    v = p.emit("tensor", lambda e, c=c, pb=pb, b=wu_buf:
                               e.matmul(
                                   out=P2[:, 512 * pb:512 * pb + 512],
                                   lhsT=qTt[:, 128 * c:128 * c + 128],
                                   rhs=b[:, 512 * c:512 * c + 512],
                                   start=(c == 0), stop=(c == 7)),
                               inc="sPE" if c == 7 else None)
                piece_pe[pb] = v
                p.wait("scalar", "sPE", v)
                p.wait("scalar", "sPool", sm_v)
                p.emit("scalar", lambda e, pb=pb: e.activation(
                    out=gtmp[:], in_=P1[:, 512 * pb:512 * pb + 512],
                    func=AF.Silu, scale=dsg[:]))
                ev = p.emit("scalar", lambda e, pb=pb: e.activation(
                    out=utmp[:], in_=P2[:, 512 * pb:512 * pb + 512],
                    func=AF.Identity, scale=dsu[:]), inc="sACT")
                gu_evac[pb] = ev
                p.wait("vector", "sACT", ev)
                gv = p.emit("vector", lambda e, pp=pp: e.tensor_tensor(
                    out=gu[:, 512 * pp:512 * pp + 512], in0=gtmp[:],
                    in1=utmp[:], op=ALU.mult), inc="sDVE")
                gu_dves.append(gv)
            p.wait("vector", "sPE", xq_used)
            am2 = p.emit("vector", lambda e: e.reduce_max(
                out=amx2[:], in_=gu[:], axis=AX.X,
                apply_absolute_value=True), inc="sDVE")
            am2 = dve_fence([amx2[:]], am2)
            p.wait("gpsimd", "sDVE", am2)
            p.emit("gpsimd", lambda e: e.tensor_single_scalar(
                out=amx2[:], in_=amx2[:], scalar=1e-5, op=ALU.max))
            p.emit("gpsimd", lambda e: e.tensor_tensor(
                out=sa[:], in0=amx2[:], in1=none1b[:], op=ALU.pow))
            p.emit("gpsimd", lambda e: e.tensor_scalar(
                out=sa[:], in0=sa[:], scalar1=127.0, scalar2=None,
                op0=ALU.mult))
            sd_v = p.emit("gpsimd", lambda e: e.tensor_scalar(
                out=ddn[:], in0=amx2[:], scalar1=mwdb[:], scalar2=1.0 / 127.0,
                op0=ALU.mult, op1=ALU.mult), inc="sPool")
            p.wait("gpsimd", "sPool", sd_v)
            sd_v = p.emit("gpsimd", lambda e: e.tensor_scalar(
                out=sa[:], in0=sa[:], scalar1=1.0, scalar2=None,
                op0=ALU.mult), inc="sPool")
            sd_v = p.emit("gpsimd", lambda e: e.tensor_scalar(
                out=ddn[:], in0=ddn[:], scalar1=1.0, scalar2=None,
                op0=ALU.mult), inc="sPool")
            if debug and t == dbg_t:
                dbg_tap("d_gu", gu, "sDVE", sd_v)
            tv = None
            for j in range(4):
                qv = quant_round(gu[:, 1024 * j:1024 * j + 1024],
                                 gu[:, 1024 * j:1024 * j + 1024], sa, C,
                                 scale_val=sd_v)
                p.wait("tensor", "sDVE", qv)
                tv = transpose_pack(gu, guqT, src_off=1024 * j,
                                    dst_off=1024 * j)
            if debug and t == dbg_t:
                dbg_tap("d_guq", gu, "sDVE", tv)
            for hn in range(2):
                p.wait("tensor", "sACT", dn_evac[hn])
                p.wait("tensor", "sDVE", tv)
                for jj in range(32):
                    v = p.emit("tensor", lambda e, jj=jj, hn=hn: e.matmul(
                        out=P3[:, 512 * hn:512 * hn + 512],
                        lhsT=guqT[:, 128 * jj:128 * jj + 128],
                        rhs=wd8[:, 1024 * jj + 512 * hn:
                                1024 * jj + 512 * hn + 512],
                        start=(jj == 0), stop=(jj == 31)),
                        inc="sPE" if jj == 31 else None)
                p.wait("scalar", "sPE", v)
                p.wait("scalar", "sPool", sd_v)
                dn_evac[hn] = p.emit("scalar", lambda e, hn=hn: e.activation(
                    out=attn[:, 512 * hn:512 * hn + 512],
                    in_=P3[:, 512 * hn:512 * hn + 512],
                    func=AF.Identity, scale=ddn[:]), inc="sACT")
            xq_used = p.now("sPE")
            p.wait("scalar", "sACT", dn_evac[1])
            sv = ln_stats(attn)
            pvv = ln_smalls(sv, 1e-5)
            t5_v = ln_apply(pvv, attn, rtmp)
            if t > 0:
                p.wait("vector", "sSt", p.now("sSt"))
            yv = p.emit("vector", lambda e, xb=xb: e.tensor_tensor(
                out=rest[:], in0=rtmp[:], in1=xb[:], op=ALU.add), inc="sDVE")
            d_free.append(yv)
            p.wait("sync", "sDVE", yv)
            p.emit("sync", lambda e, t=t: e.dma_start(
                out=y_d[128 * t:128 * t + 128, :], in_=rest[:]), inc="sSt",
                amt=16)
            if debug and t == dbg_t:
                dbg_tap("d_t4", attn, "sACT", dn_evac[1])
                dbg_tap("d_y", rest, "sDVE", yv)

        p.wait("sync", "sSt", p.now("sSt"))
        p.run()
    return dbg


def _host_prep(inputs):
    def wcat(wname, bname):
        return np.ascontiguousarray(np.concatenate(
            [np.asarray(inputs[wname]).T,
             np.asarray(inputs[bname])[None, :]], axis=0).astype(np.float32))

    wq = wcat("L0_attn_Wq", "L0_attn_bq")
    wk = wcat("L0_attn_Wk", "L0_attn_bk")
    wv = wcat("L0_attn_Wv", "L0_attn_bv")
    wg = np.ascontiguousarray(np.asarray(inputs["L0_ffn_w_gate"]).T.astype(np.float32))
    wu = np.ascontiguousarray(np.asarray(inputs["L0_ffn_w_up"]).T.astype(np.float32))
    wd = np.ascontiguousarray(np.asarray(inputs["L0_ffn_w_down"]).T.astype(np.float32))
    gamma = np.ascontiguousarray(np.asarray(inputs["L0_attn_gamma"])[None, :].astype(np.float32))
    beta = np.ascontiguousarray(np.asarray(inputs["L0_attn_beta"])[None, :].astype(np.float32))
    inv_freq = 1.0 / THETA ** (np.arange(0, C, 2, dtype=np.float64) / C)
    ang = np.arange(N, dtype=np.float64)[:, None] * inv_freq
    tabs = np.ascontiguousarray(
        np.concatenate([np.cos(ang), np.sin(ang)], axis=1).astype(np.float32))
    idn = np.eye(128, dtype=np.float32).astype(ml_dtypes.bfloat16)
    idnf = np.eye(128, dtype=np.float32)
    ones1b = np.ones((1, 128), dtype=np.float32).astype(ml_dtypes.bfloat16)
    onesf = np.ones((128, 1), dtype=np.float32)
    ones1f = np.ones((1, 128), dtype=np.float32)
    return dict(wq=wq, wk=wk, wv=wv, wg=wg, wu=wu, wd=wd, gamma=gamma,
                beta=beta, tabs=tabs, idn=idn, idnf=idnf, ones1b=ones1b,
                onesf=onesf, ones1f=ones1f)


def kernel(**inputs):
    shared = _host_prep(inputs)
    x = np.ascontiguousarray(np.asarray(inputs["L0_attn_x"]).astype(np.float32))
    nc = bass.Bass("TRN2", target_bir_lowering=False)
    build_program(nc, barriers=False)
    in_maps = []
    for b in range(8):
        m = dict(shared)
        m["x"] = np.ascontiguousarray(x[b])
        in_maps.append(m)
    res = run_bass_kernel_spmd(nc, in_maps, core_ids=list(range(8)))
    return np.stack([np.asarray(r["y"]) for r in res.results], axis=0)


# version 06
# speedup vs baseline: 1.2037x; 1.2037x over previous
import numpy as np
import ml_dtypes
import concourse.bass as bass
import concourse.mybir as mybir
from concourse.bass_utils import run_bass_kernel_spmd

F32 = mybir.dt.float32
BF16 = mybir.dt.bfloat16
FP8 = mybir.dt.float8e4
I32 = mybir.dt.int32
AF = mybir.ActivationFunctionType
ALU = mybir.AluOpType
AX = mybir.AxisListType

N = 4096
C = 1024
HEADS = 16
INNER = 4096
NT = N // 128
THETA = 10000.0

ENGINES = ("sync", "tensor", "vector", "scalar", "gpsimd")


class Plan:

    def __init__(self, nc):
        self.nc = nc
        self.q = {e: [] for e in ENGINES}
        self.sems = {}
        self.cnt = {}

    def emit(self, eng, make, inc=None, amt=1):
        if inc is not None:
            self.cnt[inc] += amt
            val = self.cnt[inc]
        else:
            val = None

        def fn(e, make=make, inc=inc, amt=amt):
            r = make(e)
            if inc is not None:
                r.then_inc(self.sems[inc], amt)
        self.q[eng].append(fn)
        return val

    def wait(self, eng, sem, val):
        if val is None or val <= 0:
            return
        self.q[eng].append(lambda e, s=sem, v=val: e.wait_ge(self.sems[s], v))

    def now(self, sem):
        return self.cnt[sem]

    def run(self):
        with self.nc.Block() as block:
            for name in ENGINES:
                fns = self.q[name]

                def body(eng, fns=fns):
                    for fn in fns:
                        fn(eng)
                getattr(block, name)(body)


def build_program(nc, n_tiles=NT, debug=False, phases="PABD", dbg_t=0, barriers=True):
    def din(name, shape, dt):
        return nc.dram_tensor(name, shape, dt, kind="ExternalInput").ap()

    def dout(name, shape, dt):
        return nc.dram_tensor(name, shape, dt, kind="ExternalOutput").ap()

    def dint(name, shape, dt):
        return nc.dram_tensor(name, shape, dt, kind="Internal").ap()

    x_d = din("x", [N, C], F32)
    wq_d = din("wq", [1025, C], F32)
    wk_d = din("wk", [1025, C], F32)
    wv_d = din("wv", [1025, C], F32)
    wg_d = din("wg", [C, INNER], F32)
    wu_d = din("wu", [C, INNER], F32)
    wd_d = din("wd", [INNER, C], F32)
    gamma_d = din("gamma", [1, C], F32)
    beta_d = din("beta", [1, C], F32)
    tabs_d = din("tabs", [N, C], F32)
    idn_d = din("idn", [128, 128], BF16)
    idnf_d = din("idnf", [128, 128], F32)
    ones1b_d = din("ones1b", [1, 128], BF16)
    onesf_d = din("onesf", [128, 1], F32)
    ones1f_d = din("ones1f", [1, 128], F32)

    y_d = dout("y", [N, C], F32)
    x2_d = dout("x2d", [N, C], F32) if debug else dint("x2d", [N, C], F32)
    wg8_d = dint("wg8d", [8, 8, 128, 512], FP8)
    wu8_d = dint("wu8d", [8, 8, 128, 512], FP8)

    dbg = {}
    if debug:
        for nm in ["a_x", "a_stats", "a_lnb", "a_kg", "a_ropeb", "a_vb", "kvS", "kmB",
                   "b_qb", "b_zpre", "b_zt", "b_attn", "b_res", "b_t2", "b_x2",
                   "d_ln3", "d_xq", "d_gu", "d_guq", "d_t4", "d_y"]:
            shape = [128, 8 * 128] if nm == "kvS" else (
                [128, 16] if nm in ("kmB", "b_zt", "b_zpre", "a_stats") else (
                    [128, INNER] if nm in ("d_gu", "d_guq") else [128, C]))
            dbg[nm] = dout("dbg_" + nm, shape, F32)

    from contextlib import ExitStack
    with ExitStack() as _st:
        _E = _st.enter_context
        wqb = _E(nc.sbuf_tensor("wqb", [128, 8 * C], BF16))
        wkb = _E(nc.sbuf_tensor("wkb", [128, 8 * C], BF16))
        wvb = _E(nc.sbuf_tensor("wvb", [128, 8 * C], BF16))
        bqr = _E(nc.sbuf_tensor("bqr", [1, C], BF16))
        bkr = _E(nc.sbuf_tensor("bkr", [1, C], BF16))
        bvr = _E(nc.sbuf_tensor("bvr", [1, C], BF16))
        wd8 = _E(nc.sbuf_tensor("wd8", [128, 32 * C], FP8))
        gammaB = _E(nc.sbuf_tensor("gammaB", [128, C], F32))
        betaB = _E(nc.sbuf_tensor("betaB", [128, C], F32))
        kvS = _E(nc.sbuf_tensor("kvS", [128, 8 * 128], BF16))
        kmB = _E(nc.sbuf_tensor("kmB", [128, 16], BF16))
        ksacc = _E(nc.sbuf_tensor("ksacc", [128, C], F32))
        idn = _E(nc.sbuf_tensor("idn_s", [128, 128], BF16))
        idnf = _E(nc.sbuf_tensor("idnf_s", [128, 128], F32))
        ones1b = _E(nc.sbuf_tensor("ones1b_s", [1, 128], BF16))
        onesf = _E(nc.sbuf_tensor("onesf_s", [128, 1], F32))
        ones1f = _E(nc.sbuf_tensor("ones1f_s", [1, 128], F32))
        halfb = _E(nc.sbuf_tensor("halfb", [128, 1], F32))
        nhalfb = _E(nc.sbuf_tensor("nhalfb", [128, 1], F32))
        wjunk = _E(nc.sbuf_tensor("wjunk", [128, 1], F32))
        none1b = _E(nc.sbuf_tensor("none1b", [128, 1], F32))
        varp = _E(nc.sbuf_tensor("varp", [128, 2], F32))
        eps5b = _E(nc.sbuf_tensor("eps5b", [128, 1], F32))
        eps12b = _E(nc.sbuf_tensor("eps12b", [128, 1], F32))
        mwgb = _E(nc.sbuf_tensor("mwgb", [128, 1], F32))
        mwub = _E(nc.sbuf_tensor("mwub", [128, 1], F32))
        mwdb = _E(nc.sbuf_tensor("mwdb", [128, 1], F32))
        swgb = _E(nc.sbuf_tensor("swgb", [128, 1], F32))
        swub = _E(nc.sbuf_tensor("swub", [128, 1], F32))
        swdb = _E(nc.sbuf_tensor("swdb", [128, 1], F32))
        xb0 = _E(nc.sbuf_tensor("xb0", [128, C], F32))
        xb1 = _E(nc.sbuf_tensor("xb1", [128, C], F32))
        tb0 = _E(nc.sbuf_tensor("tb0", [128, C], F32))
        tb1 = _E(nc.sbuf_tensor("tb1", [128, C], F32))
        wp0 = xb0
        wp1 = xb1
        qc = _E(nc.sbuf_tensor("qc", [128, C], I32))
        lnb = _E(nc.sbuf_tensor("lnb", [128, C], BF16))
        lnT = _E(nc.sbuf_tensor("lnT", [128, C], BF16))
        sqscr = _E(nc.sbuf_tensor("sqscr", [128, C], F32))
        kg = _E(nc.sbuf_tensor("kg", [128, C], F32))
        rtmp = _E(nc.sbuf_tensor("rtmp", [128, C], F32))
        rt1 = _E(nc.sbuf_tensor("rt1", [128, C], F32))
        qt = rtmp
        qd = rt1
        ropeb = _E(nc.sbuf_tensor("ropeb", [128, C], BF16))
        vbt = _E(nc.sbuf_tensor("vbt", [128, C], BF16))
        qbt = _E(nc.sbuf_tensor("qbt", [128, C], BF16))
        qTt = _E(nc.sbuf_tensor("qTt", [128, C], BF16))
        qrT = _E(nc.sbuf_tensor("qrT", [128, C], BF16))
        ztt = _E(nc.sbuf_tensor("ztt", [128, 16], F32))
        attn = _E(nc.sbuf_tensor("attn", [128, C], F32))
        rest = _E(nc.sbuf_tensor("rest", [128, C], F32))
        x2t = _E(nc.sbuf_tensor("x2t", [128, C], F32))
        wgp0 = _E(nc.sbuf_tensor("wgp0", [128, INNER], FP8))
        wgp1 = _E(nc.sbuf_tensor("wgp1", [128, INNER], FP8))
        wup0 = _E(nc.sbuf_tensor("wup0", [128, INNER], FP8))
        wup1 = _E(nc.sbuf_tensor("wup1", [128, INNER], FP8))
        gtmp = _E(nc.sbuf_tensor("gtmp", [128, 512], F32))
        utmp = _E(nc.sbuf_tensor("utmp", [128, 512], F32))
        gu = _E(nc.sbuf_tensor("gu", [128, INNER], BF16))
        guqT = _E(nc.sbuf_tensor("guqT", [128, INNER], BF16))
        ssum = _E(nc.sbuf_tensor("ssum", [128, 2], F32))
        sqsum = _E(nc.sbuf_tensor("sqsum", [128, 2], F32))
        mm_ = _E(nc.sbuf_tensor("mm_", [128, 2], F32))
        msq = _E(nc.sbuf_tensor("msq", [128, 2], F32))
        varr = _E(nc.sbuf_tensor("varr", [128, 2], F32))
        rstd = _E(nc.sbuf_tensor("rstd", [128, 2], F32))
        amax = _E(nc.sbuf_tensor("amax", [128, 1], F32))
        amx2 = _E(nc.sbuf_tensor("amx2", [128, 1], F32))
        sa = _E(nc.sbuf_tensor("sa", [128, 1], F32))
        dsg = _E(nc.sbuf_tensor("dsg", [128, 1], F32))
        dsu = _E(nc.sbuf_tensor("dsu", [128, 1], F32))
        ddn = _E(nc.sbuf_tensor("ddn", [128, 1], F32))
        sabsg = _E(nc.sbuf_tensor("sabsg", [128, 1], F32))
        sabsu = _E(nc.sbuf_tensor("sabsu", [128, 1], F32))
        sabsd = _E(nc.sbuf_tensor("sabsd", [128, 1], F32))
        acc1 = _E(nc.sbuf_tensor("acc1", [128, 1], F32))
        row1 = _E(nc.sbuf_tensor("row1", [1, C], F32))
        tiny = _E(nc.sbuf_tensor("tiny", [1, 8], F32))
        P1 = _E(nc.psum_tensor("P1", [128, C], F32))
        P2 = _E(nc.psum_tensor("P2", [128, C], F32))
        P3 = _E(nc.psum_tensor("P3", [128, C], F32))
        P4 = _E(nc.psum_tensor("P4", [128, C], BF16))
        P5 = _E(nc.psum_tensor("P5", [128, 512], F32))
        _sL = _E(nc.semaphore("sL"))
        _sLG = _E(nc.semaphore("sLG"))
        _sPE = _E(nc.semaphore("sPE"))
        _sDVE = _E(nc.semaphore("sDVE"))
        _sACT = _E(nc.semaphore("sACT"))
        _sPool = _E(nc.semaphore("sPool"))
        _sX2 = _E(nc.semaphore("sX2"))
        _sXA = _E(nc.semaphore("sXA"))
        _sXB = _E(nc.semaphore("sXB"))
        _sTA = _E(nc.semaphore("sTA"))
        _sTB = _E(nc.semaphore("sTB"))
        _sG0 = _E(nc.semaphore("sG0"))
        _sG1 = _E(nc.semaphore("sG1"))
        _sU0 = _E(nc.semaphore("sU0"))
        _sU1 = _E(nc.semaphore("sU1"))
        _sRow = _E(nc.semaphore("sRow"))
        _sW8 = _E(nc.semaphore("sW8"))
        _sSt = _E(nc.semaphore("sSt"))
        p = Plan(nc)
        p.sems = dict(sL=_sL, sLG=_sLG, sPE=_sPE, sDVE=_sDVE, sACT=_sACT,
                      sPool=_sPool, sX2=_sX2, sW8=_sW8, sSt=_sSt,
                      sXA=_sXA, sXB=_sXB, sTA=_sTA, sTB=_sTB,
                      sG0=_sG0, sG1=_sG1, sU0=_sU0, sU1=_sU1, sRow=_sRow)
        p.cnt = {k: 0 for k in p.sems}

        inv_c = 1.0 / C
        state = {"p4_free": 0, "ln_read": 0, "ln_read_pool": 0, "q_free": 0, "ln_apply": 0}

        ln_gate = {0: {"pool": 0, "apply": 0}, 1: {"pool": 0, "apply": 0}}
        ln_i = [1]

        def ln_stats(x_tile):
            i = ln_i[0] = 1 - ln_i[0]
            s_ = ssum[:, i:i + 1]
            q_ = sqsum[:, i:i + 1]
            p.wait("scalar", "sPool", ln_gate[i]["pool"])
            p.emit("scalar", lambda e, s_=s_: e.activation(
                out=sqscr[:], in_=x_tile[:], func=AF.Identity, accum_out=s_))
            v = p.emit("scalar", lambda e, q_=q_: e.activation(
                out=sqscr[:], in_=x_tile[:], func=AF.Square, accum_out=q_),
                inc="sACT")
            return i, act_fence([s_, q_], v)

        def ln_smalls(sv, eps):
            i, act_val = sv
            s_ = ssum[:, i:i + 1]
            q_ = sqsum[:, i:i + 1]
            m_ = mm_[:, i:i + 1]
            x_ = msq[:, i:i + 1]
            v_ = varr[:, i:i + 1]
            r_ = rstd[:, i:i + 1]
            p.wait("gpsimd", "sACT", act_val)
            p.wait("gpsimd", "sDVE", ln_gate[i]["apply"])
            p.emit("gpsimd", lambda e, m_=m_, s_=s_: e.tensor_scalar(
                out=m_, in0=s_, scalar1=inv_c, scalar2=None, op0=ALU.mult))
            p.emit("gpsimd", lambda e, v_=v_, q_=q_, eps=eps: e.tensor_scalar(
                out=v_, in0=q_, scalar1=inv_c, scalar2=eps,
                op0=ALU.mult, op1=ALU.add))
            p.emit("gpsimd", lambda e, x_=x_, m_=m_: e.tensor_tensor(
                out=x_, in0=m_, in1=m_, op=ALU.mult))
            p.emit("gpsimd", lambda e, v_=v_, x_=x_: e.tensor_tensor(
                out=v_, in0=v_, in1=x_, op=ALU.subtract))
            pv = p.emit("gpsimd", lambda e, r_=r_, v_=v_: e.tensor_tensor(
                out=r_, in0=v_, in1=nhalfb[:], op=ALU.pow), inc="sPool")
            ln_gate[i]["pool"] = pv
            p.wait("gpsimd", "sPool", pv)
            pv = p.emit("gpsimd", lambda e, r_=r_: e.tensor_scalar(
                out=r_, in0=r_, scalar1=1.0, scalar2=None, op0=ALU.mult))
            pv = p.emit("gpsimd", lambda e, m_=m_: e.tensor_scalar(
                out=m_, in0=m_, scalar1=1.0, scalar2=None, op0=ALU.mult),
                inc="sPool")
            ln_gate[i]["pool"] = pv
            return (i, pv)

        def ln_apply(pv, x_tile, out_tile):
            i, pool_val = pv
            m_ = mm_[:, i:i + 1]
            r_ = rstd[:, i:i + 1]
            p.wait("vector", "sPool", pool_val)
            v = p.emit("vector", lambda e, m_=m_, r_=r_: e.tensor_scalar(
                out=out_tile[:], in0=x_tile[:], scalar1=m_, scalar2=r_,
                op0=ALU.subtract, op1=ALU.mult), inc="sDVE")
            ln_gate[i]["apply"] = v
            return v

        def dve_fence(tiles, v0):
            p.wait("vector", "sDVE", v0)
            p.emit("vector", lambda e: e.engine_nop())
            p.emit("vector", lambda e: e.engine_nop())
            v = None
            for i, tl in enumerate(tiles):
                v = p.emit("vector", lambda e, tl=tl: e.tensor_scalar(
                    out=tl, in0=tl, scalar1=1.0, scalar2=None, op0=ALU.mult),
                    inc="sDVE" if i == len(tiles) - 1 else None)
            return v

        def act_fence(tiles, v0):
            p.wait("scalar", "sACT", v0)
            p.emit("scalar", lambda e: e.activation(
                out=wjunk[:], in_=halfb[:], func=AF.Identity))
            p.emit("scalar", lambda e: e.activation(
                out=wjunk[:], in_=halfb[:], func=AF.Identity))
            v = None
            for i, tl in enumerate(tiles):
                v = p.emit("scalar", lambda e, tl=tl: e.activation(
                    out=tl, in_=tl, func=AF.Identity),
                    inc="sACT" if i == len(tiles) - 1 else None)
            return v

        def transpose_pack(src_bf16, dst_bf16, src_off=0, dst_off=0):
            p.wait("tensor", "sDVE", state["p4_free"])
            for i in range(8):
                v = p.emit("tensor", lambda e, i=i: e.transpose(
                    out=P4[:, 128 * i:128 * i + 128],
                    in_=src_bf16[:, src_off + 128 * i:src_off + 128 * i + 128],
                    identity=idn[:]),
                    inc="sPE" if i == 7 else None)
            p.wait("vector", "sPE", v)
            dv = p.emit("vector", lambda e: e.tensor_copy(
                dst_bf16[:, dst_off:dst_off + C], P4[:]), inc="sDVE")
            state["p4_free"] = dv
            return dv

        def proj(wb, brow, psum, lnT_val):
            p.wait("tensor", "sDVE", lnT_val)
            pe_val = None
            for h in range(2):
                for c in range(8):
                    p.emit("tensor", lambda e, h=h, c=c, wb=wb: e.matmul(
                        out=psum[:, 512 * h:512 * h + 512],
                        lhsT=lnT[:, 128 * c:128 * c + 128],
                        rhs=wb[:, C * c + 512 * h:C * c + 512 * h + 512],
                        start=(c == 0), stop=False))
                pe_val = p.emit("tensor", lambda e, h=h, brow=brow: e.matmul(
                    out=psum[:, 512 * h:512 * h + 512],
                    lhsT=ones1b[:],
                    rhs=brow[:, 512 * h:512 * h + 512],
                    start=False, stop=True), inc="sPE")
            return pe_val

        def rope(src_f32, out_bf16, tab):
            cosv = tab[:, 0:512].rearrange("p (a b) -> p a b", b=1).broadcast_to(
                [128, 512, 2])
            sinv = tab[:, 512:1024].rearrange("p (a b) -> p a b", b=1).broadcast_to(
                [128, 512, 2])
            p.emit("vector", lambda e: e.tensor_scalar(
                out=rtmp[:, 0:C:2], in0=src_f32[:, 1:C:2], scalar1=-1.0,
                scalar2=-0.21, op0=ALU.mult, op1=ALU.add))
            p.emit("vector", lambda e: e.tensor_scalar(
                out=rtmp[:, 1:C:2], in0=src_f32[:, 0:C:2], scalar1=0.21,
                scalar2=None, op0=ALU.add))
            p.emit("vector", lambda e, cosv=cosv: e.scalar_tensor_tensor(
                out=rt1[:].rearrange("p (a b) -> p a b", b=2),
                in0=src_f32[:].rearrange("p (a b) -> p a b", b=2),
                scalar=0.21, in1=cosv, op0=ALU.add, op1=ALU.mult))
            p.emit("vector", lambda e, sinv=sinv: e.tensor_tensor(
                out=rtmp[:].rearrange("p (a b) -> p a b", b=2),
                in0=rtmp[:].rearrange("p (a b) -> p a b", b=2),
                in1=sinv, op=ALU.mult))
            return p.emit("vector", lambda e: e.tensor_tensor(
                out=out_bf16[:], in0=rt1[:], in1=rtmp[:], op=ALU.add),
                inc="sDVE")

        def quant_round(src_ap, dst_ap, scale_ap, width, scale_val=None):
            p.wait("scalar", "sDVE", state["q_free"])
            if scale_val is not None:
                p.wait("scalar", "sPool", scale_val)
            av = p.emit("scalar", lambda e: e.activation(
                out=qt[:, :width], in_=src_ap, func=AF.Identity,
                scale=scale_ap[:], bias=halfb[:]), inc="sACT")
            p.wait("vector", "sACT", av)
            p.emit("vector", lambda e: e.tensor_copy(qc[:, :width], qt[:, :width]))
            p.emit("vector", lambda e: e.tensor_tensor(
                out=qd[:, :width], in0=qt[:, :width], in1=qc[:, :width],
                op=ALU.is_lt))
            v = p.emit("vector", lambda e: e.tensor_tensor(
                out=dst_ap, in0=qc[:, :width], in1=qd[:, :width],
                op=ALU.subtract), inc="sDVE")
            state["q_free"] = v
            return v

        def dbg_tap(name, tile, eng_sem, val):
            if not debug or name not in dbg:
                return
            p.wait("gpsimd", eng_sem, val)
            p.emit("gpsimd", lambda e, name=name, tile=tile: e.dma_start(
                out=dbg[name], in_=tile[:]), inc="sSt", amt=16)

        def dbg_barrier():
            if not debug:
                return
            v = p.now("sSt")
            for e_ in ENGINES:
                p.wait(e_, "sSt", v)

        def tile_barrier():
            if not barriers:
                return
            for s_ in p.sems:
                v = p.now(s_)
                for e_ in ENGINES:
                    p.wait(e_, s_, v)

        p.emit("gpsimd", lambda e: e.memset(halfb[:], 0.5))
        p.emit("gpsimd", lambda e: e.memset(nhalfb[:], -0.5))
        p.emit("gpsimd", lambda e: e.memset(none1b[:], -1.0))
        p.emit("gpsimd", lambda e: e.memset(eps5b[:], 1e-5))
        p.emit("gpsimd", lambda e: e.memset(eps12b[:], 1e-12))
        p.emit("gpsimd", lambda e: e.memset(kvS[:], 0.0))
        p.emit("gpsimd", lambda e: e.memset(kmB[:], 0.0))
        p.emit("gpsimd", lambda e: e.memset(sabsg[:], 0.0))
        p.emit("gpsimd", lambda e: e.memset(sabsu[:], 0.0))
        p.emit("gpsimd", lambda e: e.memset(sabsd[:], 0.0))
        mz = p.emit("gpsimd", lambda e: e.memset(ksacc[:], 0.0), inc="sPool")
        p.wait("scalar", "sPool", mz)
        p.emit("scalar", lambda e: e.activation(
            out=wjunk[:], in_=halfb[:], func=AF.Abs))

        for t_, s_ in [(idn, idn_d), (idnf, idnf_d), (ones1b, ones1b_d),
                       (onesf, onesf_d), (ones1f, ones1f_d)]:
            p.emit("sync", lambda e, t_=t_, s_=s_: e.dma_start(out=t_[:], in_=s_),
                   inc="sL", amt=16)
        base_l = p.now("sL")

        for src_d, dst in [(gamma_d, gammaB), (beta_d, betaB)]:
            lv = p.emit("sync", lambda e, src_d=src_d: e.dma_start(
                out=row1[:], in_=src_d), inc="sRow", amt=16)
            p.wait("tensor", "sRow", lv)
            p.wait("tensor", "sL", base_l)
            for h in range(2):
                v = p.emit("tensor", lambda e, h=h: e.matmul(
                    out=P5[:], lhsT=ones1f[:],
                    rhs=row1[:, 512 * h:512 * h + 512],
                    start=True, stop=True), inc="sPE")
                p.wait("scalar", "sPE", v)
                av = p.emit("scalar", lambda e, h=h, dst=dst: e.copy(
                    dst[:, 512 * h:512 * h + 512], P5[:]), inc="sACT")
                p.wait("tensor", "sACT", av)
            p.wait("sync", "sACT", p.now("sACT"))

        cast_acts = []
        for wi, (wdram, wtile) in enumerate([(wk_d, wkb), (wv_d, wvb),
                                             (wq_d, wqb)]):
            for c in range(8):
                i = len(cast_acts)
                buf = wp0 if i % 2 == 0 else wp1
                if i >= 2:
                    p.wait("sync", "sACT", cast_acts[i - 2])
                bsem = "sXA" if i % 2 == 0 else "sXB"
                lv = p.emit("sync", lambda e, wdram=wdram, c=c, buf=buf:
                            e.dma_start(out=buf[:],
                                        in_=wdram[128 * c:128 * c + 128, :]),
                            inc=bsem, amt=16)
                p.wait("scalar", bsem, lv)
                av = p.emit("scalar", lambda e, wtile=wtile, c=c, buf=buf:
                            e.copy(wtile[:, C * c:C * c + C], buf[:]),
                            inc="sACT")
                cast_acts.append(av)
        for wdram, btile in [(wk_d, bkr), (wv_d, bvr), (wq_d, bqr)]:
            p.wait("sync", "sACT", p.now("sACT"))
            lv = p.emit("sync", lambda e, wdram=wdram: e.dma_start(
                out=row1[:], in_=wdram[1024:1025, :]), inc="sRow", amt=16)
            p.wait("scalar", "sRow", lv)
            p.emit("scalar", lambda e, btile=btile: e.copy(btile[:], row1[:]),
                   inc="sACT")
        qkv_ready = p.now("sACT")

        def wchunks(wdram, rows, cols):
            out = []
            for r in range(rows // 128):
                for j in range(cols // 1024):
                    out.append((wdram[128 * r:128 * r + 128,
                                      1024 * j:1024 * j + 1024], r, j))
            return out

        pool0 = p.now("sPool")
        skip_ffn = "D" not in phases
        abs_jobs = [] if skip_ffn else ([(a_, sabsg) for a_, r, j in wchunks(wg_d, C, INNER)]
                    + [(a_, sabsu) for a_, r, j in wchunks(wu_d, C, INNER)]
                    + [(a_, sabsd) for a_, r, j in wchunks(wd_d, INNER, C)])
        p.wait("gpsimd", "sACT", qkv_ready)
        abs_acts = []
        abs_dves = []
        for i, (src_ap, acc_t) in enumerate(abs_jobs):
            buf = wp0 if i % 2 == 0 else wp1
            if i >= 2:
                p.wait("gpsimd", "sACT", abs_acts[i - 2])
            bsem = "sXA" if i % 2 == 0 else "sXB"
            lv = p.emit("gpsimd", lambda e, s=src_ap, b=buf: e.dma_start(
                out=b[:], in_=s), inc=bsem, amt=16)
            p.wait("scalar", bsem, lv)
            p.wait("scalar", "sPool", pool0)
            av = p.emit("scalar", lambda e, b=buf: e.activation(
                out=sqscr[:], in_=b[:], func=AF.Abs, accum_out=acc1[:]),
                inc="sACT")
            abs_acts.append(av)
            p.wait("vector", "sACT", av)
            dv = p.emit("vector", lambda e, acc_t=acc_t: e.tensor_tensor(
                out=acc_t[:], in0=acc_t[:], in1=acc1[:], op=ALU.add),
                inc="sDVE")
            abs_dves.append(dv)

        _fv = dve_fence([sabsg[:], sabsu[:], sabsd[:]], p.now("sDVE"))
        p.wait("tensor", "sDVE", _fv)
        p.wait("tensor", "sL", base_l)
        for acc_t, mwt, swt in ([] if skip_ffn else
                               [(sabsg, mwgb, swgb), (sabsu, mwub, swub),
                                (sabsd, mwdb, swdb)]):
            v = p.emit("tensor", lambda e, acc_t=acc_t: e.matmul(
                out=P5[0:1, 0:1], lhsT=acc_t[:], rhs=onesf[:],
                start=True, stop=True), inc="sPE")
            p.wait("vector", "sPE", v)
            dv = p.emit("vector", lambda e: e.tensor_copy(
                tiny[:, 2:3], P5[0:1, 0:1].broadcast_to([1, 1])), inc="sDVE")
            p.wait("gpsimd", "sDVE", dv)
            p.emit("gpsimd", lambda e: e.tensor_scalar(
                out=tiny[:, 0:1], in0=tiny[:, 2:3],
                scalar1=1.0 / (C * INNER), scalar2=1e-5,
                op0=ALU.mult, op1=ALU.max))
            p.emit("gpsimd", lambda e: e.tensor_tensor(
                out=tiny[:, 1:2], in0=tiny[:, 0:1], in1=none1b[0:1, 0:1],
                op=ALU.pow), inc="sPool")
            pv0 = p.now("sPool")
            p.wait("gpsimd", "sPool", pv0)
            dv = p.emit("gpsimd", lambda e: e.tensor_scalar(
                out=tiny[:, 0:2], in0=tiny[:, 0:2], scalar1=1.0, scalar2=None,
                op0=ALU.mult), inc="sPool")
            p.wait("tensor", "sPool", dv)
            p.emit("tensor", lambda e: e.matmul(
                out=P5[:, 0:1], lhsT=ones1f[:], rhs=tiny[:, 0:1],
                start=True, stop=True))
            v = p.emit("tensor", lambda e: e.matmul(
                out=P5[:, 1:2], lhsT=ones1f[:], rhs=tiny[:, 1:2],
                start=True, stop=True), inc="sPE")
            p.wait("scalar", "sPE", v)
            p.emit("scalar", lambda e, mwt=mwt: e.copy(mwt[:], P5[:, 0:1]))
            av = p.emit("scalar", lambda e, swt=swt: e.copy(swt[:], P5[:, 1:2]),
                        inc="sACT")
            p.wait("tensor", "sACT", av)

        q_jobs = [] if skip_ffn else (
                  [(a_, swgb, "g", r, j) for a_, r, j in wchunks(wg_d, C, INNER)]
                  + [(a_, swub, "u", r, j) for a_, r, j in wchunks(wu_d, C, INNER)]
                  + [(a_, swdb, "d", r, j) for a_, r, j in wchunks(wd_d, INNER, C)])
        q_done = []
        q_stores = []
        p.wait("gpsimd", "sACT", p.now("sACT"))
        for i, (src_ap, sw_t, kind, r, j) in enumerate(q_jobs):
            buf = wp0 if i % 2 == 0 else wp1
            if i >= 2:
                p.wait("gpsimd", "sACT", q_tp[i - 2])
            bsem = "sXA" if i % 2 == 0 else "sXB"
            lv = p.emit("gpsimd", lambda e, s=src_ap, b=buf: e.dma_start(
                out=b[:], in_=s), inc=bsem, amt=16)
            p.wait("scalar", bsem, lv)
            p.wait("scalar", "sDVE", state["q_free"])
            av = p.emit("scalar", lambda e, b=buf, sw_t=sw_t: e.activation(
                out=qt[:], in_=b[:], func=AF.Identity, scale=sw_t[:],
                bias=halfb[:]), inc="sACT")
            if i == 0:
                q_tp = [av]
            else:
                q_tp.append(av)
            p.wait("vector", "sACT", av)
            p.emit("vector", lambda e: e.tensor_copy(qc[:], qt[:]))
            p.emit("vector", lambda e: e.tensor_tensor(
                out=qd[:], in0=qt[:], in1=qc[:], op=ALU.is_lt))
            p.emit("vector", lambda e: e.tensor_tensor(
                out=qt[:], in0=qc[:], in1=qd[:], op=ALU.subtract))
            if kind == "d":
                v = p.emit("vector", lambda e, r=r: e.tensor_scalar(
                    out=wd8[:, C * r:C * r + C], in0=qt[:], scalar1=1.0,
                    scalar2=-1.0, op0=ALU.min, op1=ALU.max), inc="sDVE")
                state["q_free"] = v
                q_done.append(v)
                q_stores.append(None)
            else:
                slot = i % 4
                if i >= 4 and q_stores[i - 4] is not None:
                    p.wait("vector", "sW8", p.now("sW8"))
                v = p.emit("vector", lambda e, slot=slot: e.tensor_scalar(
                    out=wgp1[:, C * slot:C * slot + C], in0=qt[:], scalar1=1.0,
                    scalar2=-1.0, op0=ALU.min, op1=ALU.max), inc="sDVE")
                state["q_free"] = v
                q_done.append(v)
                w8d = wg8_d if kind == "g" else wu8_d
                p.wait("sync", "sDVE", v)
                sv = None
                for pp in range(2):
                    pc = 2 * j + pp
                    sv = p.emit("sync",
                                lambda e, w8d=w8d, pc=pc, r=r, slot=slot, pp=pp:
                                e.dma_start(
                                    out=w8d[pc, r],
                                    in_=wgp1[:, C * slot + 512 * pp:
                                            C * slot + 512 * pp + 512]),
                                inc="sW8", amt=16)
                q_stores.append(sv)
        wd8_ready = p.now("sDVE")
        w8_all = p.now("sW8")

        p.emit("scalar", lambda e: e.activation(
            out=wjunk[:], in_=halfb[:], func=AF.Gelu))
        p.wait("tensor", "sACT", qkv_ready)
        p.wait("vector", "sPool", mz)
        p.wait("sync", "sACT", p.now("sACT"))
        p.wait("sync", "sDVE", p.now("sDVE"))
        a_free = []
        kproj_evac = 0
        vproj_evac = 0
        kv_started = False
        for t in range(n_tiles):
            if t == dbg_t + 1:
                dbg_barrier()
            tile_barrier()
            xb = xb0 if t % 2 == 0 else xb1
            tb = tb0 if t % 2 == 0 else tb1
            if t >= 2:
                p.wait("sync", "sDVE", a_free[t - 2])
            xsem = "sXA" if t % 2 == 0 else "sXB"
            tsem = "sTA" if t % 2 == 0 else "sTB"
            lx = p.emit("sync", lambda e, t=t, xb=xb: e.dma_start(
                out=xb[:], in_=x_d[128 * t:128 * t + 128, :]), inc=xsem, amt=16)
            lt = p.emit("sync", lambda e, t=t, tb=tb: e.dma_start(
                out=tb[:], in_=tabs_d[128 * t:128 * t + 128, :]), inc=tsem,
                amt=16)
            p.wait("scalar", xsem, lx)
            sv = ln_stats(xb)
            pvv = ln_smalls(sv, 1e-5)
            lnb_v = ln_apply(pvv, xb, lnb)
            p.wait("tensor", "sDVE", lnb_v)
            lnT_v = transpose_pack(lnb, lnT)
            if t > 0:
                p.wait("tensor", "sACT", kproj_evac)
            kp = proj(wkb, bkr, P1, lnT_v)
            p.wait("scalar", "sPE", kp)
            kproj_evac = p.emit("scalar", lambda e: e.activation(
                out=kg[:], in_=P1[:], func=AF.Gelu), inc="sACT")
            if t > 0:
                p.wait("tensor", "sACT", vproj_evac)
            vp = proj(wvb, bvr, P2, lnT_v)
            p.wait("scalar", "sPE", vp)
            vproj_evac = p.emit("scalar", lambda e: e.activation(
                out=vbt[:], in_=P2[:], func=AF.Identity, scale=1.0 / N),
                inc="sACT")
            p.wait("vector", "sACT", kproj_evac)
            p.wait("vector", tsem, lt)
            if t > 0:
                p.wait("vector", "sPE", kv_pe_prev)
            rope_v = rope(kg, ropeb, tb)
            ks_v = p.emit("vector", lambda e: e.scalar_tensor_tensor(
                out=ksacc[:], in0=kg[:], scalar=0.21, in1=ksacc[:],
                op0=ALU.add, op1=ALU.add), inc="sDVE")
            a_free.append(ks_v)
            p.wait("tensor", "sDVE", rope_v)
            p.wait("tensor", "sACT", vproj_evac)
            for h in range(HEADS):
                first = (not kv_started) and (h == 0 or h == 8)
                last = (t == n_tiles - 1) and (h == 7 or h == 15)
                p.emit("tensor", lambda e, h=h, first=first, last=last:
                       e.matmul(out=P3[0:64, 64 * h:64 * h + 64],
                                lhsT=ropeb[:, 64 * h:64 * h + 64],
                                rhs=vbt[:, 64 * h:64 * h + 64],
                                start=first, stop=last),
                       inc="sPE" if h == HEADS - 1 else None)
            kv_pe_prev = p.now("sPE")
            kv_started = True
            if debug and t == dbg_t:
                dbg_tap("a_x", xb, xsem, lx)
                p.wait("gpsimd", "sDVE", lnb_v)
                p.emit("gpsimd", lambda e: e.tensor_copy(ztt[:, 0:1], ssum[:, 0:1]))
                p.emit("gpsimd", lambda e: e.tensor_copy(ztt[:, 1:2], sqsum[:, 0:1]))
                p.emit("gpsimd", lambda e: e.tensor_copy(ztt[:, 2:3], mm_[:, 0:1]))
                p.emit("gpsimd", lambda e: e.tensor_copy(ztt[:, 3:4], rstd[:, 0:1]))
                p.emit("gpsimd", lambda e: e.tensor_copy(ztt[:, 4:5], varp[:, 0:1]))
                sdbg = p.emit("gpsimd", lambda e: e.tensor_copy(ztt[:, 5:6], varr[:, 0:1]), inc="sPool")
                dbg_tap("a_stats", ztt, "sPool", sdbg)
                dbg_tap("a_lnb", lnb, "sDVE", lnb_v)
                dbg_tap("a_kg", kg, "sACT", kproj_evac)
                dbg_tap("a_ropeb", ropeb, "sDVE", rope_v)
                dbg_tap("a_vb", vbt, "sACT", vproj_evac)
        kv_done = p.now("sPE")

        p.wait("scalar", "sPE", kv_done)
        for c in range(8):
            h0, h1 = 2 * c, 2 * c + 1
            p.emit("scalar", lambda e, c=c, h0=h0: e.copy(
                kvS[0:64, 128 * c:128 * c + 64],
                P3[0:64, 64 * h0:64 * h0 + 64]))
            p.emit("scalar", lambda e, c=c, h1=h1: e.copy(
                kvS[64:128, 128 * c + 64:128 * c + 128],
                P3[0:64, 64 * h1:64 * h1 + 64]),
                inc="sACT" if c == 7 else None)
        kvS_ready = act_fence([kvS[:]], p.now("sACT"))

        _fv = dve_fence([ksacc[:]], p.now("sDVE"))
        p.wait("tensor", "sDVE", _fv)
        for h in range(2):
            v = p.emit("tensor", lambda e, h=h: e.matmul(
                out=P5[0:1, :], lhsT=onesf[:],
                rhs=ksacc[:, 512 * h:512 * h + 512],
                start=True, stop=True), inc="sPE")
            p.wait("scalar", "sPE", v)
            av = p.emit("scalar", lambda e, h=h: e.activation(
                out=row1[:, 512 * h:512 * h + 512], in_=P5[0:1, :],
                func=AF.Identity, scale=1.0 / N), inc="sACT")
            p.wait("tensor", "sACT", av)
        _fv = act_fence([row1[:]], p.now("sACT"))
        p.wait("tensor", "sACT", _fv)
        for c in range(8):
            v = p.emit("tensor", lambda e, c=c: e.transpose(
                out=P5[:, 0:1], in_=row1[0:1, 128 * c:128 * c + 128],
                identity=ones1f[0:1, 0:1]), inc="sPE")
            p.wait("scalar", "sPE", v)
            p.emit("scalar", lambda e, c=c: e.copy(
                kmB[0:64, 2 * c:2 * c + 1], P5[0:64, 0:1]))
            av = p.emit("scalar", lambda e, c=c: e.copy(
                kmB[64:128, 2 * c + 1:2 * c + 2], P5[64:128, 0:1]),
                inc="sACT")
            p.wait("tensor", "sACT", av)
        kmB_ready = act_fence([kmB[:]], p.now("sACT"))
        if debug:
            dbg_tap("kvS", kvS, "sACT", kvS_ready)
            dbg_tap("kmB", kmB, "sACT", kmB_ready)

        if "B" not in phases:
            n_tiles_b = 0
        else:
            n_tiles_b = n_tiles
        b_free = []
        qproj_evac = 0
        z_evac = 0
        o_evac = 0
        for t in range(n_tiles_b):
            if t == dbg_t + 1:
                dbg_barrier()
            tile_barrier()
            xb = xb0 if t % 2 == 0 else xb1
            tb = tb0 if t % 2 == 0 else tb1
            if t >= 2:
                p.wait("sync", "sDVE", b_free[t - 2])
            elif t == 0:
                p.wait("sync", "sACT", kvS_ready)
            xsem = "sXA" if t % 2 == 0 else "sXB"
            tsem = "sTA" if t % 2 == 0 else "sTB"
            lx = p.emit("sync", lambda e, t=t, xb=xb: e.dma_start(
                out=xb[:], in_=x_d[128 * t:128 * t + 128, :]), inc=xsem, amt=16)
            lt = p.emit("sync", lambda e, t=t, tb=tb: e.dma_start(
                out=tb[:], in_=tabs_d[128 * t:128 * t + 128, :]), inc=tsem,
                amt=16)
            p.wait("scalar", xsem, lx)
            sv = ln_stats(xb)
            pvv = ln_smalls(sv, 1e-5)
            lnb_v = ln_apply(pvv, xb, lnb)
            p.wait("tensor", "sDVE", lnb_v)
            lnT_v = transpose_pack(lnb, lnT)
            if t > 0:
                p.wait("tensor", "sACT", qproj_evac)
            qp = proj(wqb, bqr, P1, lnT_v)
            p.wait("scalar", "sPE", qp)
            qproj_evac = p.emit("scalar", lambda e: e.activation(
                out=kg[:], in_=P1[:], func=AF.Gelu), inc="sACT")
            p.wait("vector", "sACT", qproj_evac)
            qb_v = p.emit("vector", lambda e: e.tensor_scalar(
                out=qbt[:], in0=kg[:], scalar1=0.21, scalar2=None,
                op0=ALU.add), inc="sDVE")
            p.wait("vector", tsem, lt)
            rope_v = rope(kg, ropeb, tb)
            p.wait("tensor", "sDVE", qb_v)
            qT_v = transpose_pack(qbt, qTt)
            p.wait("tensor", "sDVE", rope_v)
            qrT_v = transpose_pack(ropeb, qrT)
            if t == 0:
                p.wait("tensor", "sACT", kmB_ready)
            else:
                p.wait("tensor", "sDVE", z_evac)
            for c in range(8):
                v = p.emit("tensor", lambda e, c=c: e.matmul(
                    out=P3[:, 2 * c:2 * c + 2],
                    lhsT=qTt[:, 128 * c:128 * c + 128],
                    rhs=kmB[:, 2 * c:2 * c + 2],
                    start=(c == 0), stop=(c == 7)),
                    inc="sPE" if c == 7 else None)
            p.wait("vector", "sPE", v)
            zpre_v = p.emit("vector", lambda e: e.tensor_scalar(
                out=ztt[:], in0=P3[:, 0:16], scalar1=1e-6, scalar2=None,
                op0=ALU.add), inc="sDVE")
            if debug and t == dbg_t:
                dbg_tap("b_zpre", ztt, "sDVE", zpre_v)
            p.wait("vector", "sDVE", zpre_v)
            p.emit("vector", lambda e: e.engine_nop())
            p.emit("vector", lambda e: e.engine_nop())
            z_evac = p.emit("vector", lambda e: e.reciprocal(ztt[:], ztt[:]),
                            inc="sDVE")
            p.wait("vector", "sDVE", z_evac)
            p.emit("vector", lambda e: e.engine_nop())
            p.emit("vector", lambda e: e.engine_nop())
            if t == 0:
                p.wait("tensor", "sACT", kvS_ready)
            else:
                p.wait("tensor", "sDVE", o_evac)
            p.wait("tensor", "sDVE", qrT_v)
            for c in range(8):
                v = p.emit("tensor", lambda e, c=c: e.matmul(
                    out=P2[:, 128 * c:128 * c + 128],
                    lhsT=qrT[:, 128 * c:128 * c + 128],
                    rhs=kvS[:, 128 * c:128 * c + 128],
                    start=(c == 0 or c == 4), stop=(c == 3 or c == 7)),
                    inc="sPE" if c == 7 else None)
            p.wait("vector", "sPE", v)
            o_evac = p.emit("vector", lambda e: e.tensor_tensor(
                out=attn[:].rearrange("p (h d) -> p h d", d=64),
                in0=P2[:].rearrange("p (h d) -> p h d", d=64),
                in1=ztt[:].rearrange("p (h d) -> p h d", d=1).broadcast_to(
                    [128, 16, 64]),
                op=ALU.mult), inc="sDVE")
            p.wait("gpsimd", "sDVE", o_evac)
            res_v = p.emit("gpsimd", lambda e, xb=xb: e.tensor_tensor(
                out=rest[:], in0=attn[:], in1=xb[:], op=ALU.add), inc="sPool")
            p.wait("scalar", "sPool", res_v)
            sv = ln_stats(rest)
            pvv = ln_smalls(sv, 1e-12)
            t1_v = ln_apply(pvv, rest, attn)
            p.emit("vector", lambda e: e.tensor_tensor(
                out=attn[:], in0=attn[:], in1=gammaB[:], op=ALU.mult))
            t2_v = p.emit("vector", lambda e: e.tensor_tensor(
                out=rest[:], in0=attn[:], in1=betaB[:], op=ALU.add),
                inc="sDVE")
            p.wait("scalar", "sDVE", t2_v)
            sv = ln_stats(rest)
            pvv = ln_smalls(sv, 1e-5)
            t3_v = ln_apply(pvv, rest, attn)
            if t > 0:
                p.wait("vector", "sX2", p.now("sX2"))
            x2_v = p.emit("vector", lambda e, xb=xb: e.tensor_tensor(
                out=x2t[:], in0=attn[:], in1=xb[:], op=ALU.add), inc="sDVE")
            b_free.append(x2_v)
            p.wait("sync", "sDVE", x2_v)
            p.emit("sync", lambda e, t=t: e.dma_start(
                out=x2_d[128 * t:128 * t + 128, :], in_=x2t[:]), inc="sX2",
                amt=16)
            if debug and t == dbg_t:
                dbg_tap("b_qb", qbt, "sDVE", qb_v)
                dbg_tap("b_zt", ztt, "sDVE", z_evac)
                dbg_tap("b_attn", attn, "sDVE", o_evac)
                dbg_tap("b_res", rest, "sPool", res_v)
                dbg_tap("b_x2", x2t, "sDVE", x2_v)
        x2_all = p.now("sX2")

        n_tiles_d = n_tiles if "D" in phases else 0
        p.emit("scalar", lambda e: e.activation(
            out=wjunk[:], in_=halfb[:], func=AF.Silu))
        p.wait("sync", "sX2", x2_all)
        p.wait("tensor", "sDVE", wd8_ready)
        p.wait("gpsimd", "sW8", w8_all)
        d_free = []
        dn_evac = {0: 0, 1: 0}
        gu_evac = {0: 0, 1: 0}
        piece_pe = {0: 0, 1: 0}
        piece_next = [0]
        piece_ld = {}
        xq_used = 0
        for t in range(n_tiles_d):
            if t == dbg_t + 1:
                dbg_barrier()
            tile_barrier()
            xb = xb0 if t % 2 == 0 else xb1
            if t >= 2:
                p.wait("sync", "sDVE", d_free[t - 2])
            xsem = "sXA" if t % 2 == 0 else "sXB"
            lx = p.emit("sync", lambda e, t=t, xb=xb: e.dma_start(
                out=xb[:], in_=x2_d[128 * t:128 * t + 128, :]), inc=xsem,
                amt=16)
            p.wait("scalar", xsem, lx)
            sv = ln_stats(xb)
            pvv = ln_smalls(sv, 1e-5)
            ln3_v = ln_apply(pvv, xb, kg)
            am_v = p.emit("vector", lambda e: e.reduce_max(
                out=amax[:], in_=kg[:], axis=AX.X,
                apply_absolute_value=True), inc="sDVE")
            am_v = dve_fence([amax[:]], am_v)
            p.wait("gpsimd", "sDVE", am_v)
            p.emit("gpsimd", lambda e: e.tensor_single_scalar(
                out=amax[:], in_=amax[:], scalar=1e-5, op=ALU.max))
            p.emit("gpsimd", lambda e: e.tensor_tensor(
                out=sa[:], in0=amax[:], in1=none1b[:], op=ALU.pow))
            p.emit("gpsimd", lambda e: e.tensor_scalar(
                out=sa[:], in0=sa[:], scalar1=127.0, scalar2=None,
                op0=ALU.mult))
            p.emit("gpsimd", lambda e: e.tensor_scalar(
                out=dsg[:], in0=amax[:], scalar1=mwgb[:], scalar2=1.0 / 127.0,
                op0=ALU.mult, op1=ALU.mult))
            sm_v = p.emit("gpsimd", lambda e: e.tensor_scalar(
                out=dsu[:], in0=amax[:], scalar1=mwub[:], scalar2=1.0 / 127.0,
                op0=ALU.mult, op1=ALU.mult), inc="sPool")
            p.wait("gpsimd", "sPool", sm_v)
            sm_v = p.emit("gpsimd", lambda e: e.tensor_scalar(
                out=sa[:], in0=sa[:], scalar1=1.0, scalar2=None,
                op0=ALU.mult), inc="sPool")
            sm_v = p.emit("gpsimd", lambda e: e.tensor_scalar(
                out=dsg[:], in0=dsg[:], scalar1=1.0, scalar2=None,
                op0=ALU.mult), inc="sPool")
            sm_v = p.emit("gpsimd", lambda e: e.tensor_scalar(
                out=dsu[:], in0=dsu[:], scalar1=1.0, scalar2=None,
                op0=ALU.mult), inc="sPool")
            xq_v = quant_round(kg[:], lnb[:], sa, C, scale_val=sm_v)
            p.wait("tensor", "sDVE", xq_v)
            xqT_v = transpose_pack(lnb, qTt)
            if debug and t == dbg_t:
                dbg_tap("d_ln3", kg, "sDVE", ln3_v)
                dbg_tap("d_xq", lnb, "sDVE", xq_v)
            gu_dves = []
            for pp in range(8):
                pb = pp % 2
                wg_buf = wgp0 if pb == 0 else wgp1
                wu_buf = wup0 if pb == 0 else wup1
                gsem = "sG0" if pb == 0 else "sG1"
                usem = "sU0" if pb == 0 else "sU1"
                while piece_next[0] <= t * 8 + pp + 1 and piece_next[0] < n_tiles_d * 8:
                    gpp = piece_next[0]
                    npb = gpp % 2
                    nbg = wgp0 if npb == 0 else wgp1
                    nbu = wup0 if npb == 0 else wup1
                    ngs = "sG0" if npb == 0 else "sG1"
                    nus = "sU0" if npb == 0 else "sU1"
                    p.wait("scalar", "sPE", piece_pe[npb])
                    piece_ld[gpp] = (
                        p.emit("scalar", lambda e, gpp=gpp, b=nbg: e.dma_start(
                            out=b[:].rearrange("p (c n) -> p c n", c=8),
                            in_=wg8_d[gpp % 8].rearrange("c p n -> p c n")),
                            inc=ngs, amt=16),
                        p.emit("scalar", lambda e, gpp=gpp, b=nbu: e.dma_start(
                            out=b[:].rearrange("p (c n) -> p c n", c=8),
                            in_=wu8_d[gpp % 8].rearrange("c p n -> p c n")),
                            inc=nus, amt=16))
                    piece_next[0] += 1
                lg, lw = piece_ld[t * 8 + pp]
                p.wait("tensor", gsem, lg)
                p.wait("tensor", usem, lw)
                p.wait("tensor", "sDVE", xqT_v)
                p.wait("tensor", "sACT", gu_evac[pb])
                for c in range(8):
                    p.emit("tensor", lambda e, c=c, pb=pb, b=wg_buf: e.matmul(
                        out=P1[:, 512 * pb:512 * pb + 512],
                        lhsT=qTt[:, 128 * c:128 * c + 128],
                        rhs=b[:, 512 * c:512 * c + 512],
                        start=(c == 0), stop=(c == 7)))
                for c in range(8):
                    v = p.emit("tensor", lambda e, c=c, pb=pb, b=wu_buf:
                               e.matmul(
                                   out=P2[:, 512 * pb:512 * pb + 512],
                                   lhsT=qTt[:, 128 * c:128 * c + 128],
                                   rhs=b[:, 512 * c:512 * c + 512],
                                   start=(c == 0), stop=(c == 7)),
                               inc="sPE" if c == 7 else None)
                piece_pe[pb] = v
                p.wait("scalar", "sPE", v)
                p.wait("scalar", "sPool", sm_v)
                p.emit("scalar", lambda e, pb=pb: e.activation(
                    out=gtmp[:], in_=P1[:, 512 * pb:512 * pb + 512],
                    func=AF.Silu, scale=dsg[:]))
                ev = p.emit("scalar", lambda e, pb=pb: e.activation(
                    out=utmp[:], in_=P2[:, 512 * pb:512 * pb + 512],
                    func=AF.Identity, scale=dsu[:]), inc="sACT")
                gu_evac[pb] = ev
                p.wait("vector", "sACT", ev)
                gv = p.emit("vector", lambda e, pp=pp: e.tensor_tensor(
                    out=gu[:, 512 * pp:512 * pp + 512], in0=gtmp[:],
                    in1=utmp[:], op=ALU.mult), inc="sDVE")
                gu_dves.append(gv)
            p.wait("vector", "sPE", xq_used)
            am2 = p.emit("vector", lambda e: e.reduce_max(
                out=amx2[:], in_=gu[:], axis=AX.X,
                apply_absolute_value=True), inc="sDVE")
            am2 = dve_fence([amx2[:]], am2)
            p.wait("gpsimd", "sDVE", am2)
            p.emit("gpsimd", lambda e: e.tensor_single_scalar(
                out=amx2[:], in_=amx2[:], scalar=1e-5, op=ALU.max))
            p.emit("gpsimd", lambda e: e.tensor_tensor(
                out=sa[:], in0=amx2[:], in1=none1b[:], op=ALU.pow))
            p.emit("gpsimd", lambda e: e.tensor_scalar(
                out=sa[:], in0=sa[:], scalar1=127.0, scalar2=None,
                op0=ALU.mult))
            sd_v = p.emit("gpsimd", lambda e: e.tensor_scalar(
                out=ddn[:], in0=amx2[:], scalar1=mwdb[:], scalar2=1.0 / 127.0,
                op0=ALU.mult, op1=ALU.mult), inc="sPool")
            p.wait("gpsimd", "sPool", sd_v)
            sd_v = p.emit("gpsimd", lambda e: e.tensor_scalar(
                out=sa[:], in0=sa[:], scalar1=1.0, scalar2=None,
                op0=ALU.mult), inc="sPool")
            sd_v = p.emit("gpsimd", lambda e: e.tensor_scalar(
                out=ddn[:], in0=ddn[:], scalar1=1.0, scalar2=None,
                op0=ALU.mult), inc="sPool")
            if debug and t == dbg_t:
                dbg_tap("d_gu", gu, "sDVE", sd_v)
            tv = None
            for j in range(4):
                qv = quant_round(gu[:, 1024 * j:1024 * j + 1024],
                                 gu[:, 1024 * j:1024 * j + 1024], sa, C,
                                 scale_val=sd_v)
                p.wait("tensor", "sDVE", qv)
                tv = transpose_pack(gu, guqT, src_off=1024 * j,
                                    dst_off=1024 * j)
            if debug and t == dbg_t:
                dbg_tap("d_guq", gu, "sDVE", tv)
            for hn in range(2):
                p.wait("tensor", "sACT", dn_evac[hn])
                p.wait("tensor", "sDVE", tv)
                for jj in range(32):
                    v = p.emit("tensor", lambda e, jj=jj, hn=hn: e.matmul(
                        out=P3[:, 512 * hn:512 * hn + 512],
                        lhsT=guqT[:, 128 * jj:128 * jj + 128],
                        rhs=wd8[:, 1024 * jj + 512 * hn:
                                1024 * jj + 512 * hn + 512],
                        start=(jj == 0), stop=(jj == 31)),
                        inc="sPE" if jj == 31 else None)
                p.wait("scalar", "sPE", v)
                p.wait("scalar", "sPool", sd_v)
                dn_evac[hn] = p.emit("scalar", lambda e, hn=hn: e.activation(
                    out=attn[:, 512 * hn:512 * hn + 512],
                    in_=P3[:, 512 * hn:512 * hn + 512],
                    func=AF.Identity, scale=ddn[:]), inc="sACT")
            xq_used = p.now("sPE")
            p.wait("scalar", "sACT", dn_evac[1])
            sv = ln_stats(attn)
            pvv = ln_smalls(sv, 1e-5)
            t5_v = ln_apply(pvv, attn, rtmp)
            if t > 0:
                p.wait("vector", "sSt", p.now("sSt"))
            yv = p.emit("vector", lambda e, xb=xb: e.tensor_tensor(
                out=rest[:], in0=rtmp[:], in1=xb[:], op=ALU.add), inc="sDVE")
            d_free.append(yv)
            p.wait("sync", "sDVE", yv)
            p.emit("sync", lambda e, t=t: e.dma_start(
                out=y_d[128 * t:128 * t + 128, :], in_=rest[:]), inc="sSt",
                amt=16)
            if debug and t == dbg_t:
                dbg_tap("d_t4", attn, "sACT", dn_evac[1])
                dbg_tap("d_y", rest, "sDVE", yv)

        p.wait("sync", "sSt", p.now("sSt"))
        p.run()
    return dbg


def _host_prep(inputs):
    def wcat(wname, bname):
        return np.ascontiguousarray(np.concatenate(
            [np.asarray(inputs[wname]).T,
             np.asarray(inputs[bname])[None, :]], axis=0).astype(np.float32))

    wq = wcat("L0_attn_Wq", "L0_attn_bq")
    wk = wcat("L0_attn_Wk", "L0_attn_bk")
    wv = wcat("L0_attn_Wv", "L0_attn_bv")
    wg = np.ascontiguousarray(np.asarray(inputs["L0_ffn_w_gate"]).T.astype(np.float32))
    wu = np.ascontiguousarray(np.asarray(inputs["L0_ffn_w_up"]).T.astype(np.float32))
    wd = np.ascontiguousarray(np.asarray(inputs["L0_ffn_w_down"]).T.astype(np.float32))
    gamma = np.ascontiguousarray(np.asarray(inputs["L0_attn_gamma"])[None, :].astype(np.float32))
    beta = np.ascontiguousarray(np.asarray(inputs["L0_attn_beta"])[None, :].astype(np.float32))
    inv_freq = 1.0 / THETA ** (np.arange(0, C, 2, dtype=np.float64) / C)
    ang = np.arange(N, dtype=np.float64)[:, None] * inv_freq
    tabs = np.ascontiguousarray(
        np.concatenate([np.cos(ang), np.sin(ang)], axis=1).astype(np.float32))
    idn = np.eye(128, dtype=np.float32).astype(ml_dtypes.bfloat16)
    idnf = np.eye(128, dtype=np.float32)
    ones1b = np.ones((1, 128), dtype=np.float32).astype(ml_dtypes.bfloat16)
    onesf = np.ones((128, 1), dtype=np.float32)
    ones1f = np.ones((1, 128), dtype=np.float32)
    return dict(wq=wq, wk=wk, wv=wv, wg=wg, wu=wu, wd=wd, gamma=gamma,
                beta=beta, tabs=tabs, idn=idn, idnf=idnf, ones1b=ones1b,
                onesf=onesf, ones1f=ones1f)


def kernel(**inputs):
    shared = _host_prep(inputs)
    x = np.ascontiguousarray(np.asarray(inputs["L0_attn_x"]).astype(np.float32))
    nc = bass.Bass("TRN2", target_bir_lowering=False)
    build_program(nc, barriers=False)
    in_maps = []
    for b in range(8):
        m = dict(shared)
        m["x"] = np.ascontiguousarray(x[b])
        in_maps.append(m)
    res = run_bass_kernel_spmd(nc, in_maps, core_ids=list(range(8)))
    return np.stack([np.asarray(r["y"]) for r in res.results], axis=0)


# version 07
# speedup vs baseline: 1.3097x; 1.0880x over previous
import numpy as np
import ml_dtypes
import concourse.bass as bass
import concourse.mybir as mybir
from concourse.bass_utils import run_bass_kernel_spmd

F32 = mybir.dt.float32
BF16 = mybir.dt.bfloat16
FP8 = mybir.dt.float8e4
I32 = mybir.dt.int32
AF = mybir.ActivationFunctionType
ALU = mybir.AluOpType
AX = mybir.AxisListType

N = 4096
C = 1024
HEADS = 16
INNER = 4096
NT = N // 128
THETA = 10000.0

ENGINES = ("sync", "tensor", "vector", "scalar", "gpsimd")


class Plan:

    def __init__(self, nc):
        self.nc = nc
        self.q = {e: [] for e in ENGINES}
        self.sems = {}
        self.cnt = {}

    def emit(self, eng, make, inc=None, amt=1):
        if inc is not None:
            self.cnt[inc] += amt
            val = self.cnt[inc]
        else:
            val = None

        def fn(e, make=make, inc=inc, amt=amt):
            r = make(e)
            if inc is not None:
                r.then_inc(self.sems[inc], amt)
        self.q[eng].append(fn)
        return val

    def wait(self, eng, sem, val):
        if val is None or val <= 0:
            return
        self.q[eng].append(lambda e, s=sem, v=val: e.wait_ge(self.sems[s], v))

    def now(self, sem):
        return self.cnt[sem]

    def run(self):
        with self.nc.Block() as block:
            for name in ENGINES:
                fns = self.q[name]

                def body(eng, fns=fns):
                    for fn in fns:
                        fn(eng)
                getattr(block, name)(body)


def build_program(nc, n_tiles=NT, debug=False, phases="PABD", dbg_t=0, barriers=True):
    def din(name, shape, dt):
        return nc.dram_tensor(name, shape, dt, kind="ExternalInput").ap()

    def dout(name, shape, dt):
        return nc.dram_tensor(name, shape, dt, kind="ExternalOutput").ap()

    def dint(name, shape, dt):
        return nc.dram_tensor(name, shape, dt, kind="Internal").ap()

    x_d = din("x", [N, C], F32)
    wq_d = din("wq", [1025, C], F32)
    wk_d = din("wk", [1025, C], F32)
    wv_d = din("wv", [1025, C], F32)
    wg_d = din("wg", [C, INNER], F32)
    wu_d = din("wu", [C, INNER], F32)
    wd_d = din("wd", [INNER, C], F32)
    gamma_d = din("gamma", [1, C], F32)
    beta_d = din("beta", [1, C], F32)
    tabs_d = din("tabs", [N, C], F32)
    idn_d = din("idn", [128, 128], BF16)
    idnf_d = din("idnf", [128, 128], F32)
    ones1b_d = din("ones1b", [1, 128], BF16)
    onesf_d = din("onesf", [128, 1], F32)
    ones1f_d = din("ones1f", [1, 128], F32)

    y_d = dout("y", [N, C], F32)
    x2_d = dout("x2d", [N, C], F32) if debug else dint("x2d", [N, C], F32)
    wg8_d = dint("wg8d", [8, 8, 128, 512], FP8)
    wu8_d = dint("wu8d", [8, 8, 128, 512], FP8)

    dbg = {}
    if debug:
        for nm in ["a_x", "a_stats", "a_lnb", "a_kg", "a_ropeb", "a_vb", "kvS", "kmB",
                   "b_qb", "b_zpre", "b_zt", "b_attn", "b_res", "b_t2", "b_x2",
                   "d_ln3", "d_xq", "d_gu", "d_guq", "d_t4", "d_y"]:
            shape = [128, 8 * 128] if nm == "kvS" else (
                [128, 16] if nm in ("kmB", "b_zt", "b_zpre", "a_stats") else (
                    [128, INNER] if nm in ("d_gu", "d_guq") else [128, C]))
            dbg[nm] = dout("dbg_" + nm, shape, F32)

    from contextlib import ExitStack
    with ExitStack() as _st:
        _E = _st.enter_context
        wqb = _E(nc.sbuf_tensor("wqb", [128, 8 * C], BF16))
        wkb = _E(nc.sbuf_tensor("wkb", [128, 8 * C], BF16))
        wvb = _E(nc.sbuf_tensor("wvb", [128, 8 * C], BF16))
        bqr = _E(nc.sbuf_tensor("bqr", [1, C], BF16))
        bkr = _E(nc.sbuf_tensor("bkr", [1, C], BF16))
        bvr = _E(nc.sbuf_tensor("bvr", [1, C], BF16))
        wd8 = _E(nc.sbuf_tensor("wd8", [128, 32 * C], FP8))
        gammaB = _E(nc.sbuf_tensor("gammaB", [128, C], F32))
        betaB = _E(nc.sbuf_tensor("betaB", [128, C], F32))
        kvS = _E(nc.sbuf_tensor("kvS", [128, 8 * 128], BF16))
        kmB = _E(nc.sbuf_tensor("kmB", [128, 16], BF16))
        ksacc = _E(nc.sbuf_tensor("ksacc", [128, C], F32))
        idn = _E(nc.sbuf_tensor("idn_s", [128, 128], BF16))
        idnf = _E(nc.sbuf_tensor("idnf_s", [128, 128], F32))
        ones1b = _E(nc.sbuf_tensor("ones1b_s", [1, 128], BF16))
        onesf = _E(nc.sbuf_tensor("onesf_s", [128, 1], F32))
        ones1f = _E(nc.sbuf_tensor("ones1f_s", [1, 128], F32))
        halfb = _E(nc.sbuf_tensor("halfb", [128, 1], F32))
        nhalfb = _E(nc.sbuf_tensor("nhalfb", [128, 1], F32))
        wjunk = _E(nc.sbuf_tensor("wjunk", [128, 1], F32))
        none1b = _E(nc.sbuf_tensor("none1b", [128, 1], F32))
        varp = _E(nc.sbuf_tensor("varp", [128, 3], F32))
        eps5b = _E(nc.sbuf_tensor("eps5b", [128, 1], F32))
        eps12b = _E(nc.sbuf_tensor("eps12b", [128, 1], F32))
        mwgb = _E(nc.sbuf_tensor("mwgb", [128, 1], F32))
        mwub = _E(nc.sbuf_tensor("mwub", [128, 1], F32))
        mwdb = _E(nc.sbuf_tensor("mwdb", [128, 1], F32))
        swgb = _E(nc.sbuf_tensor("swgb", [128, 1], F32))
        swub = _E(nc.sbuf_tensor("swub", [128, 1], F32))
        swdb = _E(nc.sbuf_tensor("swdb", [128, 1], F32))
        xb0 = _E(nc.sbuf_tensor("xb0", [128, C], F32))
        xb1 = _E(nc.sbuf_tensor("xb1", [128, C], F32))
        tb0 = _E(nc.sbuf_tensor("tb0", [128, C], F32))
        tb1 = _E(nc.sbuf_tensor("tb1", [128, C], F32))
        wp0 = xb0
        wp1 = xb1
        qc = _E(nc.sbuf_tensor("qc", [128, C], I32))
        lnb = _E(nc.sbuf_tensor("lnb", [128, C], BF16))
        lnb1 = _E(nc.sbuf_tensor("lnb1", [128, C], BF16))
        lnT = _E(nc.sbuf_tensor("lnT", [128, C], BF16))
        sqscr = _E(nc.sbuf_tensor("sqscr", [128, C], F32))
        kg = _E(nc.sbuf_tensor("kg", [128, C], F32))
        rtmp = _E(nc.sbuf_tensor("rtmp", [128, C], F32))
        rt1 = _E(nc.sbuf_tensor("rt1", [128, C], F32))
        qt = rtmp
        qd = rt1
        ropeb = _E(nc.sbuf_tensor("ropeb", [128, C], BF16))
        vbt = _E(nc.sbuf_tensor("vbt", [128, C], BF16))
        qbt = _E(nc.sbuf_tensor("qbt", [128, C], BF16))
        qTt = _E(nc.sbuf_tensor("qTt", [128, C], BF16))
        qrT = _E(nc.sbuf_tensor("qrT", [128, C], BF16))
        ztt = _E(nc.sbuf_tensor("ztt", [128, 16], F32))
        attn = _E(nc.sbuf_tensor("attn", [128, C], F32))
        rest = _E(nc.sbuf_tensor("rest", [128, C], F32))
        x2t = _E(nc.sbuf_tensor("x2t", [128, C], F32))
        wgp0 = _E(nc.sbuf_tensor("wgp0", [128, INNER], FP8))
        wgp1 = _E(nc.sbuf_tensor("wgp1", [128, INNER], FP8))
        wup0 = _E(nc.sbuf_tensor("wup0", [128, INNER], FP8))
        wup1 = _E(nc.sbuf_tensor("wup1", [128, INNER], FP8))
        gtmp = _E(nc.sbuf_tensor("gtmp", [128, 512], F32))
        utmp = _E(nc.sbuf_tensor("utmp", [128, 512], F32))
        gu = _E(nc.sbuf_tensor("gu", [128, INNER], BF16))
        guqT = _E(nc.sbuf_tensor("guqT", [128, INNER], BF16))
        ssum = _E(nc.sbuf_tensor("ssum", [128, 3], F32))
        sqsum = _E(nc.sbuf_tensor("sqsum", [128, 3], F32))
        mm_ = _E(nc.sbuf_tensor("mm_", [128, 3], F32))
        msq = _E(nc.sbuf_tensor("msq", [128, 3], F32))
        varr = _E(nc.sbuf_tensor("varr", [128, 3], F32))
        rstd = _E(nc.sbuf_tensor("rstd", [128, 3], F32))
        amax = _E(nc.sbuf_tensor("amax", [128, 1], F32))
        amx2 = _E(nc.sbuf_tensor("amx2", [128, 1], F32))
        sa = _E(nc.sbuf_tensor("sa", [128, 1], F32))
        dsg = _E(nc.sbuf_tensor("dsg", [128, 1], F32))
        dsu = _E(nc.sbuf_tensor("dsu", [128, 1], F32))
        ddn = _E(nc.sbuf_tensor("ddn", [128, 1], F32))
        sabsg = _E(nc.sbuf_tensor("sabsg", [128, 1], F32))
        sabsu = _E(nc.sbuf_tensor("sabsu", [128, 1], F32))
        sabsd = _E(nc.sbuf_tensor("sabsd", [128, 1], F32))
        acc1 = _E(nc.sbuf_tensor("acc1", [128, 1], F32))
        row1 = _E(nc.sbuf_tensor("row1", [1, C], F32))
        tiny = _E(nc.sbuf_tensor("tiny", [1, 8], F32))
        P1 = _E(nc.psum_tensor("P1", [128, C], F32))
        P2 = _E(nc.psum_tensor("P2", [128, C], F32))
        P3 = _E(nc.psum_tensor("P3", [128, C], F32))
        P4 = _E(nc.psum_tensor("P4", [128, C], BF16))
        P5 = _E(nc.psum_tensor("P5", [128, 512], F32))
        _sL = _E(nc.semaphore("sL"))
        _sLG = _E(nc.semaphore("sLG"))
        _sPE = _E(nc.semaphore("sPE"))
        _sDVE = _E(nc.semaphore("sDVE"))
        _sACT = _E(nc.semaphore("sACT"))
        _sPool = _E(nc.semaphore("sPool"))
        _sX2 = _E(nc.semaphore("sX2"))
        _sXA = _E(nc.semaphore("sXA"))
        _sXB = _E(nc.semaphore("sXB"))
        _sTA = _E(nc.semaphore("sTA"))
        _sTB = _E(nc.semaphore("sTB"))
        _sG0 = _E(nc.semaphore("sG0"))
        _sG1 = _E(nc.semaphore("sG1"))
        _sU0 = _E(nc.semaphore("sU0"))
        _sU1 = _E(nc.semaphore("sU1"))
        _sRow = _E(nc.semaphore("sRow"))
        _sW8 = _E(nc.semaphore("sW8"))
        _sSt = _E(nc.semaphore("sSt"))
        p = Plan(nc)
        p.sems = dict(sL=_sL, sLG=_sLG, sPE=_sPE, sDVE=_sDVE, sACT=_sACT,
                      sPool=_sPool, sX2=_sX2, sW8=_sW8, sSt=_sSt,
                      sXA=_sXA, sXB=_sXB, sTA=_sTA, sTB=_sTB,
                      sG0=_sG0, sG1=_sG1, sU0=_sU0, sU1=_sU1, sRow=_sRow)
        p.cnt = {k: 0 for k in p.sems}

        inv_c = 1.0 / C
        state = {"p4_free": 0, "ln_read": 0, "ln_read_pool": 0, "q_free": 0, "ln_apply": 0}

        ln_gate = {i: {"pool": 0, "apply": 0} for i in range(3)}
        ln_i = [2]

        def ln_stats(x_tile):
            i = ln_i[0] = (ln_i[0] + 1) % 3
            s_ = ssum[:, i:i + 1]
            q_ = sqsum[:, i:i + 1]
            p.wait("scalar", "sPool", ln_gate[i]["pool"])
            p.emit("scalar", lambda e, s_=s_: e.activation(
                out=sqscr[:], in_=x_tile[:], func=AF.Identity, accum_out=s_))
            v = p.emit("scalar", lambda e, q_=q_: e.activation(
                out=sqscr[:], in_=x_tile[:], func=AF.Square, accum_out=q_),
                inc="sACT")
            return i, act_fence([s_, q_], v)

        def ln_smalls(sv, eps):
            i, act_val = sv
            s_ = ssum[:, i:i + 1]
            q_ = sqsum[:, i:i + 1]
            m_ = mm_[:, i:i + 1]
            x_ = msq[:, i:i + 1]
            v_ = varr[:, i:i + 1]
            r_ = rstd[:, i:i + 1]
            p.wait("gpsimd", "sACT", act_val)
            p.wait("gpsimd", "sDVE", ln_gate[i]["apply"])
            p.emit("gpsimd", lambda e, m_=m_, s_=s_: e.tensor_scalar(
                out=m_, in0=s_, scalar1=inv_c, scalar2=None, op0=ALU.mult))
            p.emit("gpsimd", lambda e, v_=v_, q_=q_, eps=eps: e.tensor_scalar(
                out=v_, in0=q_, scalar1=inv_c, scalar2=eps,
                op0=ALU.mult, op1=ALU.add))
            p.emit("gpsimd", lambda e, x_=x_, m_=m_: e.tensor_tensor(
                out=x_, in0=m_, in1=m_, op=ALU.mult))
            p.emit("gpsimd", lambda e, v_=v_, x_=x_: e.tensor_tensor(
                out=v_, in0=v_, in1=x_, op=ALU.subtract))
            pv = p.emit("gpsimd", lambda e, r_=r_, v_=v_: e.tensor_tensor(
                out=r_, in0=v_, in1=nhalfb[:], op=ALU.pow), inc="sPool")
            ln_gate[i]["pool"] = pv
            p.wait("gpsimd", "sPool", pv)
            pv = p.emit("gpsimd", lambda e, r_=r_: e.tensor_scalar(
                out=r_, in0=r_, scalar1=1.0, scalar2=None, op0=ALU.mult))
            pv = p.emit("gpsimd", lambda e, m_=m_: e.tensor_scalar(
                out=m_, in0=m_, scalar1=1.0, scalar2=None, op0=ALU.mult),
                inc="sPool")
            ln_gate[i]["pool"] = pv
            return (i, pv)

        def ln_apply(pv, x_tile, out_tile):
            i, pool_val = pv
            m_ = mm_[:, i:i + 1]
            r_ = rstd[:, i:i + 1]
            p.wait("vector", "sPool", pool_val)
            v = p.emit("vector", lambda e, m_=m_, r_=r_: e.tensor_scalar(
                out=out_tile[:], in0=x_tile[:], scalar1=m_, scalar2=r_,
                op0=ALU.subtract, op1=ALU.mult), inc="sDVE")
            ln_gate[i]["apply"] = v
            return v

        def dve_fence(tiles, v0):
            p.wait("vector", "sDVE", v0)
            p.emit("vector", lambda e: e.engine_nop())
            p.emit("vector", lambda e: e.engine_nop())
            v = None
            for i, tl in enumerate(tiles):
                v = p.emit("vector", lambda e, tl=tl: e.tensor_scalar(
                    out=tl, in0=tl, scalar1=1.0, scalar2=None, op0=ALU.mult),
                    inc="sDVE" if i == len(tiles) - 1 else None)
            return v

        def act_fence(tiles, v0):
            p.wait("scalar", "sACT", v0)
            p.emit("scalar", lambda e: e.activation(
                out=wjunk[:], in_=halfb[:], func=AF.Identity))
            p.emit("scalar", lambda e: e.activation(
                out=wjunk[:], in_=halfb[:], func=AF.Identity))
            v = None
            for i, tl in enumerate(tiles):
                v = p.emit("scalar", lambda e, tl=tl: e.activation(
                    out=tl, in_=tl, func=AF.Identity),
                    inc="sACT" if i == len(tiles) - 1 else None)
            return v

        def transpose_pack(src_bf16, dst_bf16, src_off=0, dst_off=0):
            p.wait("tensor", "sDVE", state["p4_free"])
            for i in range(8):
                v = p.emit("tensor", lambda e, i=i: e.transpose(
                    out=P4[:, 128 * i:128 * i + 128],
                    in_=src_bf16[:, src_off + 128 * i:src_off + 128 * i + 128],
                    identity=idn[:]),
                    inc="sPE" if i == 7 else None)
            p.wait("vector", "sPE", v)
            dv = p.emit("vector", lambda e: e.tensor_copy(
                dst_bf16[:, dst_off:dst_off + C], P4[:]), inc="sDVE")
            state["p4_free"] = dv
            return dv

        def proj(wb, brow, psum, lnT_val):
            p.wait("tensor", "sDVE", lnT_val)
            pe_val = None
            for h in range(2):
                for c in range(8):
                    p.emit("tensor", lambda e, h=h, c=c, wb=wb: e.matmul(
                        out=psum[:, 512 * h:512 * h + 512],
                        lhsT=lnT[:, 128 * c:128 * c + 128],
                        rhs=wb[:, C * c + 512 * h:C * c + 512 * h + 512],
                        start=(c == 0), stop=False))
                pe_val = p.emit("tensor", lambda e, h=h, brow=brow: e.matmul(
                    out=psum[:, 512 * h:512 * h + 512],
                    lhsT=ones1b[:],
                    rhs=brow[:, 512 * h:512 * h + 512],
                    start=False, stop=True), inc="sPE")
            return pe_val

        def rope(src_f32, out_bf16, tab):
            cosv = tab[:, 0:512].rearrange("p (a b) -> p a b", b=1).broadcast_to(
                [128, 512, 2])
            sinv = tab[:, 512:1024].rearrange("p (a b) -> p a b", b=1).broadcast_to(
                [128, 512, 2])
            p.emit("vector", lambda e: e.tensor_scalar(
                out=rtmp[:, 0:C:2], in0=src_f32[:, 1:C:2], scalar1=-1.0,
                scalar2=-0.21, op0=ALU.mult, op1=ALU.add))
            p.emit("vector", lambda e: e.tensor_scalar(
                out=rtmp[:, 1:C:2], in0=src_f32[:, 0:C:2], scalar1=0.21,
                scalar2=None, op0=ALU.add))
            p.emit("vector", lambda e, cosv=cosv: e.scalar_tensor_tensor(
                out=rt1[:].rearrange("p (a b) -> p a b", b=2),
                in0=src_f32[:].rearrange("p (a b) -> p a b", b=2),
                scalar=0.21, in1=cosv, op0=ALU.add, op1=ALU.mult))
            p.emit("vector", lambda e, sinv=sinv: e.tensor_tensor(
                out=rtmp[:].rearrange("p (a b) -> p a b", b=2),
                in0=rtmp[:].rearrange("p (a b) -> p a b", b=2),
                in1=sinv, op=ALU.mult))
            return p.emit("vector", lambda e: e.tensor_tensor(
                out=out_bf16[:], in0=rt1[:], in1=rtmp[:], op=ALU.add),
                inc="sDVE")

        def quant_round(src_ap, dst_ap, scale_ap, width, scale_val=None):
            p.wait("scalar", "sDVE", state["q_free"])
            if scale_val is not None:
                p.wait("scalar", "sPool", scale_val)
            av = p.emit("scalar", lambda e: e.activation(
                out=qt[:, :width], in_=src_ap, func=AF.Identity,
                scale=scale_ap[:], bias=halfb[:]), inc="sACT")
            p.wait("vector", "sACT", av)
            p.emit("vector", lambda e: e.tensor_copy(qc[:, :width], qt[:, :width]))
            p.emit("vector", lambda e: e.tensor_tensor(
                out=qd[:, :width], in0=qt[:, :width], in1=qc[:, :width],
                op=ALU.is_lt))
            v = p.emit("vector", lambda e: e.tensor_tensor(
                out=dst_ap, in0=qc[:, :width], in1=qd[:, :width],
                op=ALU.subtract), inc="sDVE")
            state["q_free"] = v
            return v

        def dbg_tap(name, tile, eng_sem, val):
            if not debug or name not in dbg:
                return
            p.wait("gpsimd", eng_sem, val)
            p.emit("gpsimd", lambda e, name=name, tile=tile: e.dma_start(
                out=dbg[name], in_=tile[:]), inc="sSt", amt=16)

        def dbg_barrier():
            if not debug:
                return
            v = p.now("sSt")
            for e_ in ENGINES:
                p.wait(e_, "sSt", v)

        def tile_barrier():
            if not barriers:
                return
            for s_ in p.sems:
                v = p.now(s_)
                for e_ in ENGINES:
                    p.wait(e_, s_, v)

        p.emit("gpsimd", lambda e: e.memset(halfb[:], 0.5))
        p.emit("gpsimd", lambda e: e.memset(nhalfb[:], -0.5))
        p.emit("gpsimd", lambda e: e.memset(none1b[:], -1.0))
        p.emit("gpsimd", lambda e: e.memset(eps5b[:], 1e-5))
        p.emit("gpsimd", lambda e: e.memset(eps12b[:], 1e-12))
        p.emit("gpsimd", lambda e: e.memset(kvS[:], 0.0))
        p.emit("gpsimd", lambda e: e.memset(kmB[:], 0.0))
        p.emit("gpsimd", lambda e: e.memset(sabsg[:], 0.0))
        p.emit("gpsimd", lambda e: e.memset(sabsu[:], 0.0))
        p.emit("gpsimd", lambda e: e.memset(sabsd[:], 0.0))
        mz = p.emit("gpsimd", lambda e: e.memset(ksacc[:], 0.0), inc="sPool")
        p.wait("scalar", "sPool", mz)
        p.emit("scalar", lambda e: e.activation(
            out=wjunk[:], in_=halfb[:], func=AF.Abs))

        for t_, s_ in [(idn, idn_d), (idnf, idnf_d), (ones1b, ones1b_d),
                       (onesf, onesf_d), (ones1f, ones1f_d)]:
            p.emit("sync", lambda e, t_=t_, s_=s_: e.dma_start(out=t_[:], in_=s_),
                   inc="sL", amt=16)
        base_l = p.now("sL")

        for src_d, dst in [(gamma_d, gammaB), (beta_d, betaB)]:
            lv = p.emit("sync", lambda e, src_d=src_d: e.dma_start(
                out=row1[:], in_=src_d), inc="sRow", amt=16)
            p.wait("tensor", "sRow", lv)
            p.wait("tensor", "sL", base_l)
            for h in range(2):
                v = p.emit("tensor", lambda e, h=h: e.matmul(
                    out=P5[:], lhsT=ones1f[:],
                    rhs=row1[:, 512 * h:512 * h + 512],
                    start=True, stop=True), inc="sPE")
                p.wait("scalar", "sPE", v)
                av = p.emit("scalar", lambda e, h=h, dst=dst: e.copy(
                    dst[:, 512 * h:512 * h + 512], P5[:]), inc="sACT")
                p.wait("tensor", "sACT", av)
            p.wait("sync", "sACT", p.now("sACT"))

        cast_acts = []
        for wi, (wdram, wtile) in enumerate([(wk_d, wkb), (wv_d, wvb),
                                             (wq_d, wqb)]):
            for c in range(8):
                i = len(cast_acts)
                buf = wp0 if i % 2 == 0 else wp1
                if i >= 2:
                    p.wait("sync", "sACT", cast_acts[i - 2])
                bsem = "sXA" if i % 2 == 0 else "sXB"
                lv = p.emit("sync", lambda e, wdram=wdram, c=c, buf=buf:
                            e.dma_start(out=buf[:],
                                        in_=wdram[128 * c:128 * c + 128, :]),
                            inc=bsem, amt=16)
                p.wait("scalar", bsem, lv)
                av = p.emit("scalar", lambda e, wtile=wtile, c=c, buf=buf:
                            e.copy(wtile[:, C * c:C * c + C], buf[:]),
                            inc="sACT")
                cast_acts.append(av)
        for wdram, btile in [(wk_d, bkr), (wv_d, bvr), (wq_d, bqr)]:
            p.wait("sync", "sACT", p.now("sACT"))
            lv = p.emit("sync", lambda e, wdram=wdram: e.dma_start(
                out=row1[:], in_=wdram[1024:1025, :]), inc="sRow", amt=16)
            p.wait("scalar", "sRow", lv)
            p.emit("scalar", lambda e, btile=btile: e.copy(btile[:], row1[:]),
                   inc="sACT")
        qkv_ready = p.now("sACT")

        def wchunks(wdram, rows, cols):
            out = []
            for r in range(rows // 128):
                for j in range(cols // 1024):
                    out.append((wdram[128 * r:128 * r + 128,
                                      1024 * j:1024 * j + 1024], r, j))
            return out

        pool0 = p.now("sPool")
        skip_ffn = "D" not in phases
        abs_jobs = [] if skip_ffn else ([(a_, sabsg) for a_, r, j in wchunks(wg_d, C, INNER)]
                    + [(a_, sabsu) for a_, r, j in wchunks(wu_d, C, INNER)]
                    + [(a_, sabsd) for a_, r, j in wchunks(wd_d, INNER, C)])
        p.wait("gpsimd", "sACT", qkv_ready)
        abs_acts = []
        abs_dves = []
        for i, (src_ap, acc_t) in enumerate(abs_jobs):
            buf = wp0 if i % 2 == 0 else wp1
            if i >= 2:
                p.wait("gpsimd", "sACT", abs_acts[i - 2])
            bsem = "sXA" if i % 2 == 0 else "sXB"
            lv = p.emit("gpsimd", lambda e, s=src_ap, b=buf: e.dma_start(
                out=b[:], in_=s), inc=bsem, amt=16)
            p.wait("scalar", bsem, lv)
            p.wait("scalar", "sPool", pool0)
            av = p.emit("scalar", lambda e, b=buf: e.activation(
                out=sqscr[:], in_=b[:], func=AF.Abs, accum_out=acc1[:]),
                inc="sACT")
            abs_acts.append(av)
            p.wait("vector", "sACT", av)
            dv = p.emit("vector", lambda e, acc_t=acc_t: e.tensor_tensor(
                out=acc_t[:], in0=acc_t[:], in1=acc1[:], op=ALU.add),
                inc="sDVE")
            abs_dves.append(dv)

        _fv = dve_fence([sabsg[:], sabsu[:], sabsd[:]], p.now("sDVE"))
        p.wait("tensor", "sDVE", _fv)
        p.wait("tensor", "sL", base_l)
        for acc_t, mwt, swt in ([] if skip_ffn else
                               [(sabsg, mwgb, swgb), (sabsu, mwub, swub),
                                (sabsd, mwdb, swdb)]):
            v = p.emit("tensor", lambda e, acc_t=acc_t: e.matmul(
                out=P5[0:1, 0:1], lhsT=acc_t[:], rhs=onesf[:],
                start=True, stop=True), inc="sPE")
            p.wait("vector", "sPE", v)
            dv = p.emit("vector", lambda e: e.tensor_copy(
                tiny[:, 2:3], P5[0:1, 0:1].broadcast_to([1, 1])), inc="sDVE")
            p.wait("gpsimd", "sDVE", dv)
            p.emit("gpsimd", lambda e: e.tensor_scalar(
                out=tiny[:, 0:1], in0=tiny[:, 2:3],
                scalar1=1.0 / (C * INNER), scalar2=1e-5,
                op0=ALU.mult, op1=ALU.max))
            p.emit("gpsimd", lambda e: e.tensor_tensor(
                out=tiny[:, 1:2], in0=tiny[:, 0:1], in1=none1b[0:1, 0:1],
                op=ALU.pow), inc="sPool")
            pv0 = p.now("sPool")
            p.wait("gpsimd", "sPool", pv0)
            dv = p.emit("gpsimd", lambda e: e.tensor_scalar(
                out=tiny[:, 0:2], in0=tiny[:, 0:2], scalar1=1.0, scalar2=None,
                op0=ALU.mult), inc="sPool")
            p.wait("tensor", "sPool", dv)
            p.emit("tensor", lambda e: e.matmul(
                out=P5[:, 0:1], lhsT=ones1f[:], rhs=tiny[:, 0:1],
                start=True, stop=True))
            v = p.emit("tensor", lambda e: e.matmul(
                out=P5[:, 1:2], lhsT=ones1f[:], rhs=tiny[:, 1:2],
                start=True, stop=True), inc="sPE")
            p.wait("scalar", "sPE", v)
            p.emit("scalar", lambda e, mwt=mwt: e.copy(mwt[:], P5[:, 0:1]))
            av = p.emit("scalar", lambda e, swt=swt: e.copy(swt[:], P5[:, 1:2]),
                        inc="sACT")
            p.wait("tensor", "sACT", av)

        q_jobs = [] if skip_ffn else (
                  [(a_, swgb, "g", r, j) for a_, r, j in wchunks(wg_d, C, INNER)]
                  + [(a_, swub, "u", r, j) for a_, r, j in wchunks(wu_d, C, INNER)]
                  + [(a_, swdb, "d", r, j) for a_, r, j in wchunks(wd_d, INNER, C)])
        q_done = []
        q_stores = []
        p.wait("gpsimd", "sACT", p.now("sACT"))
        for i, (src_ap, sw_t, kind, r, j) in enumerate(q_jobs):
            buf = wp0 if i % 2 == 0 else wp1
            if i >= 2:
                p.wait("gpsimd", "sACT", q_tp[i - 2])
            bsem = "sXA" if i % 2 == 0 else "sXB"
            lv = p.emit("gpsimd", lambda e, s=src_ap, b=buf: e.dma_start(
                out=b[:], in_=s), inc=bsem, amt=16)
            p.wait("scalar", bsem, lv)
            p.wait("scalar", "sDVE", state["q_free"])
            av = p.emit("scalar", lambda e, b=buf, sw_t=sw_t: e.activation(
                out=qt[:], in_=b[:], func=AF.Identity, scale=sw_t[:],
                bias=halfb[:]), inc="sACT")
            if i == 0:
                q_tp = [av]
            else:
                q_tp.append(av)
            p.wait("vector", "sACT", av)
            p.emit("vector", lambda e: e.tensor_copy(qc[:], qt[:]))
            p.emit("vector", lambda e: e.tensor_tensor(
                out=qd[:], in0=qt[:], in1=qc[:], op=ALU.is_lt))
            p.emit("vector", lambda e: e.tensor_tensor(
                out=qt[:], in0=qc[:], in1=qd[:], op=ALU.subtract))
            if kind == "d":
                v = p.emit("vector", lambda e, r=r: e.tensor_scalar(
                    out=wd8[:, C * r:C * r + C], in0=qt[:], scalar1=1.0,
                    scalar2=-1.0, op0=ALU.min, op1=ALU.max), inc="sDVE")
                state["q_free"] = v
                q_done.append(v)
                q_stores.append(None)
            else:
                slot = i % 4
                if i >= 4 and q_stores[i - 4] is not None:
                    p.wait("vector", "sW8", p.now("sW8"))
                v = p.emit("vector", lambda e, slot=slot: e.tensor_scalar(
                    out=wgp1[:, C * slot:C * slot + C], in0=qt[:], scalar1=1.0,
                    scalar2=-1.0, op0=ALU.min, op1=ALU.max), inc="sDVE")
                state["q_free"] = v
                q_done.append(v)
                w8d = wg8_d if kind == "g" else wu8_d
                p.wait("sync", "sDVE", v)
                sv = None
                for pp in range(2):
                    pc = 2 * j + pp
                    sv = p.emit("sync",
                                lambda e, w8d=w8d, pc=pc, r=r, slot=slot, pp=pp:
                                e.dma_start(
                                    out=w8d[pc, r],
                                    in_=wgp1[:, C * slot + 512 * pp:
                                            C * slot + 512 * pp + 512]),
                                inc="sW8", amt=16)
                q_stores.append(sv)
        wd8_ready = p.now("sDVE")
        w8_all = p.now("sW8")

        p.emit("scalar", lambda e: e.activation(
            out=wjunk[:], in_=halfb[:], func=AF.Gelu))
        p.wait("tensor", "sACT", qkv_ready)
        p.wait("vector", "sPool", mz)
        p.wait("sync", "sACT", p.now("sACT"))
        p.wait("sync", "sDVE", p.now("sDVE"))
        a_free = {}
        lnb_par = [lnb, lnb1]
        a_lx, a_lt, a_pvv, a_apply_v = {}, {}, {}, {}
        kproj_evac = 0
        vproj_evac = 0
        kv_started = False
        kv_pe_prev = 0

        def a_load(t):
            if t >= n_tiles:
                return
            xb = xb0 if t % 2 == 0 else xb1
            tb = tb0 if t % 2 == 0 else tb1
            if t >= 2:
                p.wait("sync", "sDVE", a_free[t - 2])
            a_lx[t] = p.emit("sync", lambda e, t=t, xb=xb: e.dma_start(
                out=xb[:], in_=x_d[128 * t:128 * t + 128, :]),
                inc="sXA" if t % 2 == 0 else "sXB", amt=16)
            a_lt[t] = p.emit("sync", lambda e, t=t, tb=tb: e.dma_start(
                out=tb[:], in_=tabs_d[128 * t:128 * t + 128, :]),
                inc="sTA" if t % 2 == 0 else "sTB", amt=16)

        def a_chain(t):
            if t >= n_tiles:
                return
            xb = xb0 if t % 2 == 0 else xb1
            p.wait("scalar", "sXA" if t % 2 == 0 else "sXB", a_lx[t])
            sv = ln_stats(xb)
            a_pvv[t] = ln_smalls(sv, 1e-5)

        def a_apply(t):
            if t >= n_tiles:
                return
            xb = xb0 if t % 2 == 0 else xb1
            p.wait("vector", "sXA" if t % 2 == 0 else "sXB", a_lx[t])
            v = ln_apply(a_pvv[t], xb, lnb_par[t % 2])
            a_apply_v[t] = v
            a_free[t] = v

        a_load(0)
        a_load(1)
        a_chain(0)
        a_apply(0)
        for t in range(n_tiles):
            a_load(t + 2)
            a_chain(t + 1)
            p.wait("tensor", "sDVE", a_apply_v[t])
            lnT_v = transpose_pack(lnb_par[t % 2], lnT)
            if t > 0:
                p.wait("tensor", "sACT", kproj_evac)
            kp = proj(wkb, bkr, P1, lnT_v)
            p.wait("scalar", "sPE", kp)
            kproj_evac = p.emit("scalar", lambda e: e.activation(
                out=kg[:], in_=P1[:], func=AF.Gelu), inc="sACT")
            if t > 0:
                p.wait("tensor", "sACT", vproj_evac)
            vp = proj(wvb, bvr, P2, lnT_v)
            p.wait("scalar", "sPE", vp)
            vproj_evac = p.emit("scalar", lambda e: e.activation(
                out=vbt[:], in_=P2[:], func=AF.Identity, scale=1.0 / N),
                inc="sACT")
            p.wait("vector", "sACT", kproj_evac)
            p.wait("vector", "sTA" if t % 2 == 0 else "sTB", a_lt[t])
            if t > 0:
                p.wait("vector", "sPE", kv_pe_prev)
            rope_v = rope(kg, ropeb, tb0 if t % 2 == 0 else tb1)
            ks_v = p.emit("vector", lambda e: e.scalar_tensor_tensor(
                out=ksacc[:], in0=kg[:], scalar=0.21, in1=ksacc[:],
                op0=ALU.add, op1=ALU.add), inc="sDVE")
            p.wait("tensor", "sDVE", rope_v)
            p.wait("tensor", "sACT", vproj_evac)
            for h in range(HEADS):
                first = (not kv_started) and (h == 0 or h == 8)
                last = (t == n_tiles - 1) and (h == 7 or h == 15)
                p.emit("tensor", lambda e, h=h, first=first, last=last:
                       e.matmul(out=P3[0:64, 64 * h:64 * h + 64],
                                lhsT=ropeb[:, 64 * h:64 * h + 64],
                                rhs=vbt[:, 64 * h:64 * h + 64],
                                start=first, stop=last),
                       inc="sPE" if h == HEADS - 1 else None)
            kv_pe_prev = p.now("sPE")
            kv_started = True
            a_apply(t + 1)
            if debug and t == dbg_t:
                dbg_tap("a_lnb", lnb_par[t % 2], "sDVE", a_apply_v[t])
                dbg_tap("a_kg", kg, "sACT", kproj_evac)
                dbg_tap("a_ropeb", ropeb, "sDVE", rope_v)
                dbg_tap("a_vb", vbt, "sACT", vproj_evac)
        kv_done = p.now("sPE")

        p.wait("scalar", "sPE", kv_done)
        for c in range(8):
            h0, h1 = 2 * c, 2 * c + 1
            p.emit("scalar", lambda e, c=c, h0=h0: e.copy(
                kvS[0:64, 128 * c:128 * c + 64],
                P3[0:64, 64 * h0:64 * h0 + 64]))
            p.emit("scalar", lambda e, c=c, h1=h1: e.copy(
                kvS[64:128, 128 * c + 64:128 * c + 128],
                P3[0:64, 64 * h1:64 * h1 + 64]),
                inc="sACT" if c == 7 else None)
        kvS_ready = act_fence([kvS[:]], p.now("sACT"))

        _fv = dve_fence([ksacc[:]], p.now("sDVE"))
        p.wait("tensor", "sDVE", _fv)
        for h in range(2):
            v = p.emit("tensor", lambda e, h=h: e.matmul(
                out=P5[0:1, :], lhsT=onesf[:],
                rhs=ksacc[:, 512 * h:512 * h + 512],
                start=True, stop=True), inc="sPE")
            p.wait("scalar", "sPE", v)
            av = p.emit("scalar", lambda e, h=h: e.activation(
                out=row1[:, 512 * h:512 * h + 512], in_=P5[0:1, :],
                func=AF.Identity, scale=1.0 / N), inc="sACT")
            p.wait("tensor", "sACT", av)
        _fv = act_fence([row1[:]], p.now("sACT"))
        p.wait("tensor", "sACT", _fv)
        for c in range(8):
            v = p.emit("tensor", lambda e, c=c: e.transpose(
                out=P5[:, 0:1], in_=row1[0:1, 128 * c:128 * c + 128],
                identity=ones1f[0:1, 0:1]), inc="sPE")
            p.wait("scalar", "sPE", v)
            p.emit("scalar", lambda e, c=c: e.copy(
                kmB[0:64, 2 * c:2 * c + 1], P5[0:64, 0:1]))
            av = p.emit("scalar", lambda e, c=c: e.copy(
                kmB[64:128, 2 * c + 1:2 * c + 2], P5[64:128, 0:1]),
                inc="sACT")
            p.wait("tensor", "sACT", av)
        kmB_ready = act_fence([kmB[:]], p.now("sACT"))
        if debug:
            dbg_tap("kvS", kvS, "sACT", kvS_ready)
            dbg_tap("kmB", kmB, "sACT", kmB_ready)

        if "B" not in phases:
            n_tiles_b = 0
        else:
            n_tiles_b = n_tiles
        b_free = {}
        b_lx, b_lt, b_pvv, b_apply_v = {}, {}, {}, {}
        qproj_evac = 0
        z_evac = 0
        o_evac = 0

        def b_load(t):
            if t >= n_tiles_b:
                return
            xb = xb0 if t % 2 == 0 else xb1
            tb = tb0 if t % 2 == 0 else tb1
            if t >= 2:
                p.wait("sync", "sDVE", b_free[t - 2])
            b_lx[t] = p.emit("sync", lambda e, t=t, xb=xb: e.dma_start(
                out=xb[:], in_=x_d[128 * t:128 * t + 128, :]),
                inc="sXA" if t % 2 == 0 else "sXB", amt=16)
            b_lt[t] = p.emit("sync", lambda e, t=t, tb=tb: e.dma_start(
                out=tb[:], in_=tabs_d[128 * t:128 * t + 128, :]),
                inc="sTA" if t % 2 == 0 else "sTB", amt=16)

        def b_chain(t):
            if t >= n_tiles_b:
                return
            xb = xb0 if t % 2 == 0 else xb1
            p.wait("scalar", "sXA" if t % 2 == 0 else "sXB", b_lx[t])
            sv = ln_stats(xb)
            b_pvv[t] = ln_smalls(sv, 1e-5)

        def b_apply(t):
            if t >= n_tiles_b:
                return
            xb = xb0 if t % 2 == 0 else xb1
            p.wait("vector", "sXA" if t % 2 == 0 else "sXB", b_lx[t])
            v = ln_apply(b_pvv[t], xb, lnb_par[t % 2])
            b_apply_v[t] = v

        if n_tiles_b:
            p.wait("sync", "sACT", kvS_ready)
            b_load(0)
            b_load(1)
            b_chain(0)
            b_apply(0)
        for t in range(n_tiles_b):
            b_chain(t + 1)
            xb = xb0 if t % 2 == 0 else xb1
            tb = tb0 if t % 2 == 0 else tb1
            p.wait("tensor", "sDVE", b_apply_v[t])
            lnT_v = transpose_pack(lnb_par[t % 2], lnT)
            if t > 0:
                p.wait("tensor", "sACT", qproj_evac)
            qp = proj(wqb, bqr, P1, lnT_v)
            p.wait("scalar", "sPE", qp)
            qproj_evac = p.emit("scalar", lambda e: e.activation(
                out=kg[:], in_=P1[:], func=AF.Gelu), inc="sACT")
            p.wait("vector", "sACT", qproj_evac)
            qb_v = p.emit("vector", lambda e: e.tensor_scalar(
                out=qbt[:], in0=kg[:], scalar1=0.21, scalar2=None,
                op0=ALU.add), inc="sDVE")
            p.wait("vector", "sTA" if t % 2 == 0 else "sTB", b_lt[t])
            rope_v = rope(kg, ropeb, tb)
            p.wait("tensor", "sDVE", qb_v)
            qT_v = transpose_pack(qbt, qTt)
            p.wait("tensor", "sDVE", rope_v)
            qrT_v = transpose_pack(ropeb, qrT)
            if t == 0:
                p.wait("tensor", "sACT", kmB_ready)
            else:
                p.wait("tensor", "sDVE", z_evac)
            for c in range(8):
                v = p.emit("tensor", lambda e, c=c: e.matmul(
                    out=P3[:, 2 * c:2 * c + 2],
                    lhsT=qTt[:, 128 * c:128 * c + 128],
                    rhs=kmB[:, 2 * c:2 * c + 2],
                    start=(c == 0), stop=(c == 7)),
                    inc="sPE" if c == 7 else None)
            p.wait("vector", "sPE", v)
            zpre_v = p.emit("vector", lambda e: e.tensor_scalar(
                out=ztt[:], in0=P3[:, 0:16], scalar1=1e-6, scalar2=None,
                op0=ALU.add), inc="sDVE")
            if debug and t == dbg_t:
                dbg_tap("b_zpre", ztt, "sDVE", zpre_v)
            p.wait("vector", "sDVE", zpre_v)
            p.emit("vector", lambda e: e.engine_nop())
            p.emit("vector", lambda e: e.engine_nop())
            z_evac = p.emit("vector", lambda e: e.reciprocal(ztt[:], ztt[:]),
                            inc="sDVE")
            p.wait("vector", "sDVE", z_evac)
            p.emit("vector", lambda e: e.engine_nop())
            p.emit("vector", lambda e: e.engine_nop())
            if t == 0:
                p.wait("tensor", "sACT", kvS_ready)
            else:
                p.wait("tensor", "sDVE", o_evac)
            p.wait("tensor", "sDVE", qrT_v)
            for c in range(8):
                v = p.emit("tensor", lambda e, c=c: e.matmul(
                    out=P2[:, 128 * c:128 * c + 128],
                    lhsT=qrT[:, 128 * c:128 * c + 128],
                    rhs=kvS[:, 128 * c:128 * c + 128],
                    start=(c == 0 or c == 4), stop=(c == 3 or c == 7)),
                    inc="sPE" if c == 7 else None)
            p.wait("vector", "sPE", v)
            o_evac = p.emit("vector", lambda e: e.tensor_tensor(
                out=attn[:].rearrange("p (h d) -> p h d", d=64),
                in0=P2[:].rearrange("p (h d) -> p h d", d=64),
                in1=ztt[:].rearrange("p (h d) -> p h d", d=1).broadcast_to(
                    [128, 16, 64]),
                op=ALU.mult), inc="sDVE")
            p.wait("gpsimd", "sDVE", o_evac)
            res_v = p.emit("gpsimd", lambda e, xb=xb: e.tensor_tensor(
                out=rest[:], in0=attn[:], in1=xb[:], op=ALU.add), inc="sPool")
            p.wait("scalar", "sPool", res_v)
            sv = ln_stats(rest)
            pvv = ln_smalls(sv, 1e-12)
            t1_v = ln_apply(pvv, rest, attn)
            p.emit("vector", lambda e: e.tensor_tensor(
                out=attn[:], in0=attn[:], in1=gammaB[:], op=ALU.mult))
            t2_v = p.emit("vector", lambda e: e.tensor_tensor(
                out=rest[:], in0=attn[:], in1=betaB[:], op=ALU.add),
                inc="sDVE")
            p.wait("scalar", "sDVE", t2_v)
            sv = ln_stats(rest)
            pvv = ln_smalls(sv, 1e-5)
            t3_v = ln_apply(pvv, rest, attn)
            if t > 0:
                p.wait("vector", "sX2", p.now("sX2"))
            x2_v = p.emit("vector", lambda e, xb=xb: e.tensor_tensor(
                out=x2t[:], in0=attn[:], in1=xb[:], op=ALU.add), inc="sDVE")
            b_free[t] = x2_v
            p.wait("sync", "sDVE", x2_v)
            p.emit("sync", lambda e, t=t: e.dma_start(
                out=x2_d[128 * t:128 * t + 128, :], in_=x2t[:]), inc="sX2",
                amt=16)
            b_load(t + 2)
            b_apply(t + 1)
            if debug and t == dbg_t:
                dbg_tap("b_qb", qbt, "sDVE", qb_v)
                dbg_tap("b_zt", ztt, "sDVE", z_evac)
                dbg_tap("b_attn", attn, "sDVE", o_evac)
                dbg_tap("b_res", rest, "sPool", res_v)
                dbg_tap("b_x2", x2t, "sDVE", x2_v)
        x2_all = p.now("sX2")

        n_tiles_d = n_tiles if "D" in phases else 0
        p.emit("scalar", lambda e: e.activation(
            out=wjunk[:], in_=halfb[:], func=AF.Silu))
        p.wait("sync", "sX2", x2_all)
        p.wait("tensor", "sDVE", wd8_ready)
        p.wait("gpsimd", "sW8", w8_all)
        d_free = []
        dn_evac = {0: 0, 1: 0}
        gu_evac = {0: 0, 1: 0}
        piece_pe = {0: 0, 1: 0}
        piece_next = [0]
        piece_ld = {}
        xq_used = 0
        for t in range(n_tiles_d):
            if t == dbg_t + 1:
                dbg_barrier()
            tile_barrier()
            xb = xb0 if t % 2 == 0 else xb1
            if t >= 2:
                p.wait("sync", "sDVE", d_free[t - 2])
            xsem = "sXA" if t % 2 == 0 else "sXB"
            lx = p.emit("sync", lambda e, t=t, xb=xb: e.dma_start(
                out=xb[:], in_=x2_d[128 * t:128 * t + 128, :]), inc=xsem,
                amt=16)
            p.wait("scalar", xsem, lx)
            sv = ln_stats(xb)
            pvv = ln_smalls(sv, 1e-5)
            ln3_v = ln_apply(pvv, xb, kg)
            am_v = p.emit("vector", lambda e: e.reduce_max(
                out=amax[:], in_=kg[:], axis=AX.X,
                apply_absolute_value=True), inc="sDVE")
            am_v = dve_fence([amax[:]], am_v)
            p.wait("gpsimd", "sDVE", am_v)
            p.emit("gpsimd", lambda e: e.tensor_single_scalar(
                out=amax[:], in_=amax[:], scalar=1e-5, op=ALU.max))
            p.emit("gpsimd", lambda e: e.tensor_tensor(
                out=sa[:], in0=amax[:], in1=none1b[:], op=ALU.pow))
            p.emit("gpsimd", lambda e: e.tensor_scalar(
                out=sa[:], in0=sa[:], scalar1=127.0, scalar2=None,
                op0=ALU.mult))
            p.emit("gpsimd", lambda e: e.tensor_scalar(
                out=dsg[:], in0=amax[:], scalar1=mwgb[:], scalar2=1.0 / 127.0,
                op0=ALU.mult, op1=ALU.mult))
            sm_v = p.emit("gpsimd", lambda e: e.tensor_scalar(
                out=dsu[:], in0=amax[:], scalar1=mwub[:], scalar2=1.0 / 127.0,
                op0=ALU.mult, op1=ALU.mult), inc="sPool")
            p.wait("gpsimd", "sPool", sm_v)
            sm_v = p.emit("gpsimd", lambda e: e.tensor_scalar(
                out=sa[:], in0=sa[:], scalar1=1.0, scalar2=None,
                op0=ALU.mult), inc="sPool")
            sm_v = p.emit("gpsimd", lambda e: e.tensor_scalar(
                out=dsg[:], in0=dsg[:], scalar1=1.0, scalar2=None,
                op0=ALU.mult), inc="sPool")
            sm_v = p.emit("gpsimd", lambda e: e.tensor_scalar(
                out=dsu[:], in0=dsu[:], scalar1=1.0, scalar2=None,
                op0=ALU.mult), inc="sPool")
            xq_v = quant_round(kg[:], lnb[:], sa, C, scale_val=sm_v)
            p.wait("tensor", "sDVE", xq_v)
            xqT_v = transpose_pack(lnb, qTt)
            if debug and t == dbg_t:
                dbg_tap("d_ln3", kg, "sDVE", ln3_v)
                dbg_tap("d_xq", lnb, "sDVE", xq_v)
            gu_dves = []
            for pp in range(8):
                pb = pp % 2
                wg_buf = wgp0 if pb == 0 else wgp1
                wu_buf = wup0 if pb == 0 else wup1
                gsem = "sG0" if pb == 0 else "sG1"
                usem = "sU0" if pb == 0 else "sU1"
                while piece_next[0] <= t * 8 + pp + 1 and piece_next[0] < n_tiles_d * 8:
                    gpp = piece_next[0]
                    npb = gpp % 2
                    nbg = wgp0 if npb == 0 else wgp1
                    nbu = wup0 if npb == 0 else wup1
                    ngs = "sG0" if npb == 0 else "sG1"
                    nus = "sU0" if npb == 0 else "sU1"
                    p.wait("scalar", "sPE", piece_pe[npb])
                    piece_ld[gpp] = (
                        p.emit("scalar", lambda e, gpp=gpp, b=nbg: e.dma_start(
                            out=b[:].rearrange("p (c n) -> p c n", c=8),
                            in_=wg8_d[gpp % 8].rearrange("c p n -> p c n")),
                            inc=ngs, amt=16),
                        p.emit("scalar", lambda e, gpp=gpp, b=nbu: e.dma_start(
                            out=b[:].rearrange("p (c n) -> p c n", c=8),
                            in_=wu8_d[gpp % 8].rearrange("c p n -> p c n")),
                            inc=nus, amt=16))
                    piece_next[0] += 1
                lg, lw = piece_ld[t * 8 + pp]
                p.wait("tensor", gsem, lg)
                p.wait("tensor", usem, lw)
                p.wait("tensor", "sDVE", xqT_v)
                p.wait("tensor", "sACT", gu_evac[pb])
                for c in range(8):
                    p.emit("tensor", lambda e, c=c, pb=pb, b=wg_buf: e.matmul(
                        out=P1[:, 512 * pb:512 * pb + 512],
                        lhsT=qTt[:, 128 * c:128 * c + 128],
                        rhs=b[:, 512 * c:512 * c + 512],
                        start=(c == 0), stop=(c == 7)))
                for c in range(8):
                    v = p.emit("tensor", lambda e, c=c, pb=pb, b=wu_buf:
                               e.matmul(
                                   out=P2[:, 512 * pb:512 * pb + 512],
                                   lhsT=qTt[:, 128 * c:128 * c + 128],
                                   rhs=b[:, 512 * c:512 * c + 512],
                                   start=(c == 0), stop=(c == 7)),
                               inc="sPE" if c == 7 else None)
                piece_pe[pb] = v
                p.wait("scalar", "sPE", v)
                p.wait("scalar", "sPool", sm_v)
                p.emit("scalar", lambda e, pb=pb: e.activation(
                    out=gtmp[:], in_=P1[:, 512 * pb:512 * pb + 512],
                    func=AF.Silu, scale=dsg[:]))
                ev = p.emit("scalar", lambda e, pb=pb: e.activation(
                    out=utmp[:], in_=P2[:, 512 * pb:512 * pb + 512],
                    func=AF.Identity, scale=dsu[:]), inc="sACT")
                gu_evac[pb] = ev
                p.wait("vector", "sACT", ev)
                gv = p.emit("vector", lambda e, pp=pp: e.tensor_tensor(
                    out=gu[:, 512 * pp:512 * pp + 512], in0=gtmp[:],
                    in1=utmp[:], op=ALU.mult), inc="sDVE")
                gu_dves.append(gv)
            p.wait("vector", "sPE", xq_used)
            am2 = p.emit("vector", lambda e: e.reduce_max(
                out=amx2[:], in_=gu[:], axis=AX.X,
                apply_absolute_value=True), inc="sDVE")
            am2 = dve_fence([amx2[:]], am2)
            p.wait("gpsimd", "sDVE", am2)
            p.emit("gpsimd", lambda e: e.tensor_single_scalar(
                out=amx2[:], in_=amx2[:], scalar=1e-5, op=ALU.max))
            p.emit("gpsimd", lambda e: e.tensor_tensor(
                out=sa[:], in0=amx2[:], in1=none1b[:], op=ALU.pow))
            p.emit("gpsimd", lambda e: e.tensor_scalar(
                out=sa[:], in0=sa[:], scalar1=127.0, scalar2=None,
                op0=ALU.mult))
            sd_v = p.emit("gpsimd", lambda e: e.tensor_scalar(
                out=ddn[:], in0=amx2[:], scalar1=mwdb[:], scalar2=1.0 / 127.0,
                op0=ALU.mult, op1=ALU.mult), inc="sPool")
            p.wait("gpsimd", "sPool", sd_v)
            sd_v = p.emit("gpsimd", lambda e: e.tensor_scalar(
                out=sa[:], in0=sa[:], scalar1=1.0, scalar2=None,
                op0=ALU.mult), inc="sPool")
            sd_v = p.emit("gpsimd", lambda e: e.tensor_scalar(
                out=ddn[:], in0=ddn[:], scalar1=1.0, scalar2=None,
                op0=ALU.mult), inc="sPool")
            if debug and t == dbg_t:
                dbg_tap("d_gu", gu, "sDVE", sd_v)
            tv = None
            for j in range(4):
                qv = quant_round(gu[:, 1024 * j:1024 * j + 1024],
                                 gu[:, 1024 * j:1024 * j + 1024], sa, C,
                                 scale_val=sd_v)
                p.wait("tensor", "sDVE", qv)
                tv = transpose_pack(gu, guqT, src_off=1024 * j,
                                    dst_off=1024 * j)
            if debug and t == dbg_t:
                dbg_tap("d_guq", gu, "sDVE", tv)
            for hn in range(2):
                p.wait("tensor", "sACT", dn_evac[hn])
                p.wait("tensor", "sDVE", tv)
                for jj in range(32):
                    v = p.emit("tensor", lambda e, jj=jj, hn=hn: e.matmul(
                        out=P3[:, 512 * hn:512 * hn + 512],
                        lhsT=guqT[:, 128 * jj:128 * jj + 128],
                        rhs=wd8[:, 1024 * jj + 512 * hn:
                                1024 * jj + 512 * hn + 512],
                        start=(jj == 0), stop=(jj == 31)),
                        inc="sPE" if jj == 31 else None)
                p.wait("scalar", "sPE", v)
                p.wait("scalar", "sPool", sd_v)
                dn_evac[hn] = p.emit("scalar", lambda e, hn=hn: e.activation(
                    out=attn[:, 512 * hn:512 * hn + 512],
                    in_=P3[:, 512 * hn:512 * hn + 512],
                    func=AF.Identity, scale=ddn[:]), inc="sACT")
            xq_used = p.now("sPE")
            p.wait("scalar", "sACT", dn_evac[1])
            sv = ln_stats(attn)
            pvv = ln_smalls(sv, 1e-5)
            t5_v = ln_apply(pvv, attn, rtmp)
            if t > 0:
                p.wait("vector", "sSt", p.now("sSt"))
            yv = p.emit("vector", lambda e, xb=xb: e.tensor_tensor(
                out=rest[:], in0=rtmp[:], in1=xb[:], op=ALU.add), inc="sDVE")
            d_free.append(yv)
            p.wait("sync", "sDVE", yv)
            p.emit("sync", lambda e, t=t: e.dma_start(
                out=y_d[128 * t:128 * t + 128, :], in_=rest[:]), inc="sSt",
                amt=16)
            if debug and t == dbg_t:
                dbg_tap("d_t4", attn, "sACT", dn_evac[1])
                dbg_tap("d_y", rest, "sDVE", yv)

        p.wait("sync", "sSt", p.now("sSt"))
        p.run()
    return dbg


def _host_prep(inputs):
    def wcat(wname, bname):
        return np.ascontiguousarray(np.concatenate(
            [np.asarray(inputs[wname]).T,
             np.asarray(inputs[bname])[None, :]], axis=0).astype(np.float32))

    wq = wcat("L0_attn_Wq", "L0_attn_bq")
    wk = wcat("L0_attn_Wk", "L0_attn_bk")
    wv = wcat("L0_attn_Wv", "L0_attn_bv")
    wg = np.ascontiguousarray(np.asarray(inputs["L0_ffn_w_gate"]).T.astype(np.float32))
    wu = np.ascontiguousarray(np.asarray(inputs["L0_ffn_w_up"]).T.astype(np.float32))
    wd = np.ascontiguousarray(np.asarray(inputs["L0_ffn_w_down"]).T.astype(np.float32))
    gamma = np.ascontiguousarray(np.asarray(inputs["L0_attn_gamma"])[None, :].astype(np.float32))
    beta = np.ascontiguousarray(np.asarray(inputs["L0_attn_beta"])[None, :].astype(np.float32))
    inv_freq = 1.0 / THETA ** (np.arange(0, C, 2, dtype=np.float64) / C)
    ang = np.arange(N, dtype=np.float64)[:, None] * inv_freq
    tabs = np.ascontiguousarray(
        np.concatenate([np.cos(ang), np.sin(ang)], axis=1).astype(np.float32))
    idn = np.eye(128, dtype=np.float32).astype(ml_dtypes.bfloat16)
    idnf = np.eye(128, dtype=np.float32)
    ones1b = np.ones((1, 128), dtype=np.float32).astype(ml_dtypes.bfloat16)
    onesf = np.ones((128, 1), dtype=np.float32)
    ones1f = np.ones((1, 128), dtype=np.float32)
    return dict(wq=wq, wk=wk, wv=wv, wg=wg, wu=wu, wd=wd, gamma=gamma,
                beta=beta, tabs=tabs, idn=idn, idnf=idnf, ones1b=ones1b,
                onesf=onesf, ones1f=ones1f)


def kernel(**inputs):
    shared = _host_prep(inputs)
    x = np.ascontiguousarray(np.asarray(inputs["L0_attn_x"]).astype(np.float32))
    nc = bass.Bass("TRN2", target_bir_lowering=False)
    build_program(nc, barriers=False)
    in_maps = []
    for b in range(8):
        m = dict(shared)
        m["x"] = np.ascontiguousarray(x[b])
        in_maps.append(m)
    res = run_bass_kernel_spmd(nc, in_maps, core_ids=list(range(8)))
    return np.stack([np.asarray(r["y"]) for r in res.results], axis=0)
